# version 03
# baseline (speedup 1.0000x reference)
import numpy as np
import concourse.bass as bass
import concourse.mybir as mybir
from concourse.bass_utils import run_bass_kernel_spmd
from concourse.alu_op_type import AluOpType as Op

F32 = mybir.dt.float32
BF16 = mybir.dt.bfloat16
I32 = mybir.dt.int32
AF = mybir.ActivationFunctionType
AX = mybir.AxisListType

B, S, D, N, RNK, NE = 4, 2048, 4096, 16, 128, 64
DH = D // 2
TB = 512
NT = S // TB
NDC = DH // 128
NDF = D // 128
NTT = TB // 128


class Ctr:
    def __init__(self, nc, name):
        self.h = nc.alloc_semaphore(name)
        self.v = 0

    def inc(self, instr, k=1):
        instr.then_inc(self.h, k)
        self.v += k
        return self.v

    def wait(self, eng, v=None):
        eng.wait_ge(self.h, self.v if v is None else v)


def build_program():
    nc = bass.Bass("TRN2", target_bir_lowering=False)

    d_x = nc.dram_tensor("x", [S, D], F32, kind="ExternalInput")
    d_xh = nc.dram_tensor("xh", [S, DH], F32, kind="ExternalInput")
    d_wcat = nc.dram_tensor("wcat", [128, NDF * 224], F32, kind="ExternalInput")
    d_dtw = nc.dram_tensor("dtw", [RNK, DH], F32, kind="ExternalInput")
    d_alog = nc.dram_tensor("alog", [128, NDC * N], F32, kind="ExternalInput")
    d_dvec = nc.dram_tensor("dvec", [128, NDC], F32, kind="ExternalInput")
    d_bvec = nc.dram_tensor("bvec", [128, NDC], F32, kind="ExternalInput")
    d_rb = nc.dram_tensor("rb", [NE, 1], F32, kind="ExternalInput")
    d_eb = nc.dram_tensor("eb", [NE, DH], F32, kind="ExternalInput")
    d_out = nc.dram_tensor("out", [S, DH], F32, kind="ExternalOutput")

    arena = nc.alloc_sbuf_tensor("arena", [128, NTT * D], F32)
    xnT = nc.alloc_sbuf_tensor("xnT", [128, NDC * TB], F32)
    xnT16 = nc.alloc_sbuf_tensor("xnT16", [128, 2 * TB], BF16)
    h_all = nc.alloc_sbuf_tensor("h_all", [128, N * TB], F32)
    wcat16 = nc.alloc_sbuf_tensor("wcat16", [128, NDF * 224], BF16)
    dtw16 = nc.alloc_sbuf_tensor("dtw16", [128, DH], BF16)
    eb16 = nc.alloc_sbuf_tensor("eb16", [NE, DH], BF16)
    a_sb = nc.alloc_sbuf_tensor("a_sb", [128, NDC * N], F32)
    dvec = nc.alloc_sbuf_tensor("dvec_sb", [128, NDC], F32)
    bvec = nc.alloc_sbuf_tensor("bvec_sb", [128, NDC], F32)
    rb_sb = nc.alloc_sbuf_tensor("rb_sb", [NE, 1], F32)
    wd16 = nc.alloc_sbuf_tensor("wd16", [NE, TB], BF16)
    xp16 = nc.alloc_sbuf_tensor("xp16", [128, TB], BF16)
    bc32 = nc.alloc_sbuf_tensor("bc32", [32, TB], F32)
    lgrm = nc.alloc_sbuf_tensor("lgrm", [NE, TB], F32)
    lgtm = nc.alloc_sbuf_tensor("lgtm", [128, NTT * NE], F32)
    etm = nc.alloc_sbuf_tensor("etm", [128, NTT * NE], F32)
    msk = nc.alloc_sbuf_tensor("msk", [128, NTT * NE], F32)
    wdtm = nc.alloc_sbuf_tensor("wdtm", [128, NTT * 72], F32)
    t8 = nc.alloc_sbuf_tensor("t8", [128, NTT * 8], F32)
    cols = nc.alloc_sbuf_tensor("cols", [128, 48], F32)
    spare = nc.alloc_sbuf_tensor("spare", [128, 1], F32)
    wsrow = nc.alloc_sbuf_tensor("wsrow", [1, TB], F32)
    wsb = nc.alloc_sbuf_tensor("wsb", [128, TB], F32)
    dt_sb = nc.alloc_sbuf_tensor("dt_sb", [128, TB], F32)
    sp_st = nc.alloc_sbuf_tensor("sp_st", [128, TB], F32)
    u_sb = nc.alloc_sbuf_tensor("u_sb", [128, TB], F32)
    y_sb = nc.alloc_sbuf_tensor("y_sb", [128, TB], F32)
    da_sb = nc.alloc_sbuf_tensor("da_sb", [128, 2 * TB], F32)
    db_sb = nc.alloc_sbuf_tensor("db_sb", [128, 2 * TB], F32)
    hst = nc.alloc_sbuf_tensor("hst", [128, NDC * N], F32)
    xres = nc.alloc_sbuf_tensor("xres", [128, 2 * TB], F32)
    outsb = nc.alloc_sbuf_tensor("outsb", [128, 2 * TB], F32)
    idf = nc.alloc_sbuf_tensor("idf", [128, 128], F32)
    idi = nc.alloc_sbuf_tensor("idi", [128, 128], I32)
    iopf = nc.alloc_sbuf_tensor("iopf", [32, 128], I32)
    sel = nc.alloc_sbuf_tensor("sel", [32, 128], F32)
    ones = nc.alloc_sbuf_tensor("ones", [1, 128], F32)
    eps_sb = nc.alloc_sbuf_tensor("eps_sb", [128, 1], F32)

    p_tp = nc.alloc_psum_tensor("p_tp", [128, 512], F32)
    p_xp1 = nc.alloc_psum_tensor("p_xp1", [128, 512], F32)
    p_bc = nc.alloc_psum_tensor("p_bc", [32, 512], F32)
    p_lg = nc.alloc_psum_tensor("p_lg", [NE, 512], F32)
    p_dt = nc.alloc_psum_tensor("p_dt", [128, 512], F32)
    p_bm = nc.alloc_psum_tensor("p_bm", [128, 512], F32)
    p_bb = nc.alloc_psum_tensor("p_bb", [128, 512], F32)
    p_ms = nc.alloc_psum_tensor("p_ms", [128, 512], F32)

    ld = Ctr(nc, "c_ld")
    pe = Ctr(nc, "c_pe")
    act = Ctr(nc, "c_act")
    dve = Ctr(nc, "c_dve")
    st = Ctr(nc, "c_st")
    gp = Ctr(nc, "c_gp")

    nc.all_engine_barrier()
    nc.gpsimd.sem_clear(range(ld.h.num, gp.h.num + 1))
    nc.all_engine_barrier()

    V, A, T, G, SY = nc.vector, nc.scalar, nc.tensor, nc.gpsimd, nc.sync

    ld.inc(SY.dma_start(arena[:, 0 : NDF * 224], d_wcat[:, :]), 16)
    ld.inc(SY.dma_start(h_all[:, 0:DH], d_dtw[:, :]), 16)
    ld.inc(SY.dma_start(h_all[0:NE, DH : 2 * DH], d_eb[:, :]), 16)
    ld.inc(SY.dma_start(cols[0:NE, 0:1], d_rb[:, :]), 16)
    ld.inc(SY.dma_start(a_sb[:, :], d_alog[:, :]), 16)
    ld.inc(SY.dma_start(dvec[:, :], d_dvec[:, :]), 16)
    ld.inc(SY.dma_start(bvec[:, :], d_bvec[:, :]), 16)

    G.iota(idi[:], pattern=[[1, 128]], base=0, channel_multiplier=-1)
    gp.inc(G.iota(iopf[:], pattern=[[0, 128]], base=0, channel_multiplier=1))

    ld.wait(V)
    gp.wait(V)
    V.tensor_scalar(idf[:], idi[:], 0, None, Op.is_equal)
    V.memset(ones[:], 1.0)
    V.memset(eps_sb[:], 1e-5)
    V.memset(spare[:], 0.0)
    V.memset(hst[:], 0.0)
    V.tensor_copy(wcat16[:], arena[:, 0 : NDF * 224])
    V.tensor_copy(dtw16[:], h_all[:, 0:DH])
    V.tensor_copy(eb16[:], h_all[0:NE, DH : 2 * DH])
    V.tensor_copy(rb_sb[:], cols[0:NE, 0:1])
    mv = dve.inc(V.engine_nop())
    act.wait(A, 0)
    A.wait_ge(dve.h, mv)
    ma = act.inc(A.activation(a_sb[:], a_sb[:], AF.Exp, bias=0.0, scale=1.0))
    V.wait_ge(act.h, ma)
    V.tensor_scalar(a_sb[:], a_sb[:], -1.0, None, Op.mult)
    dve.inc(V.engine_nop())

    nc.all_engine_barrier()

    for tau in range(NT):
        r0 = tau * TB
        for tt in range(NTT):
            ld.inc(SY.dma_start(arena[:, tt * D : (tt + 1) * D],
                                d_x[r0 + tt * 128 : r0 + (tt + 1) * 128, :]), 16)
        m_ld_x = ld.v
        V.wait_ge(ld.h, m_ld_x)
        XT = [arena[:, tt * D : (tt + 1) * D] for tt in range(NTT)]
        cl = lambda j, tt: cols[:, 4 * j + tt : 4 * j + tt + 1]
        for tt in range(NTT):
            V.tensor_reduce(cl(0, tt), XT[tt], AX.X, Op.add)
        for tt in range(NTT):
            V.scalar_tensor_tensor(h_all[:, (tt & 1) * D : (tt & 1) * D + D],
                                   XT[tt], 1.0, XT[tt],
                                   Op.mult, Op.mult, accum_out=cl(1, tt))
        for tt in range(NTT):
            V.tensor_scalar(cl(2, tt), cl(0, tt), 1.0 / D, None, Op.mult)
        for tt in range(NTT):
            V.tensor_scalar(cl(3, tt), cl(1, tt), 1.0 / D, None, Op.mult)
        for tt in range(NTT):
            V.scalar_tensor_tensor(cl(4, tt), cl(2, tt), 1.0, cl(2, tt),
                                   Op.mult, Op.mult)
        for tt in range(NTT):
            V.tensor_tensor(cl(5, tt), cl(3, tt), cl(4, tt), Op.subtract)
        m1 = dve.inc(V.engine_nop())
        A.wait_ge(dve.h, m1)
        for tt in range(NTT):
            A.activation(cl(6, tt), cl(5, tt), AF.Ln, bias=eps_sb[:, 0:1],
                         scale=1.0)
        m2 = act.v
        for tt in range(NTT):
            m2 = act.inc(A.activation(cl(7, tt), cl(6, tt), AF.Exp, bias=0.0,
                                      scale=-0.5))
        V.wait_ge(act.h, m2)
        for tt in range(NTT):
            V.scalar_tensor_tensor(cl(8, tt), cl(2, tt), -1.0, cl(7, tt),
                                   Op.mult, Op.mult)
        for tt in range(NTT):
            V.tensor_scalar(XT[tt], XT[tt], cl(7, tt), cl(8, tt), Op.mult, Op.add)
        m3 = dve.inc(V.engine_nop())
        m_xn = dve.v
        m_mm_prev = pe.v
        for dc in range(NDF):
            slot = dc & 1
            T.wait_ge(dve.h, m_xn)
            for tt in range(NTT):
                ti = T.transpose(p_tp[:, tt * 128 : (tt + 1) * 128],
                                 arena[:, tt * D + dc * 128 : tt * D + (dc + 1) * 128],
                                 idf[:])
            m4 = pe.inc(ti)
            V.wait_ge(pe.h, m4)
            V.tensor_copy(xnT16[:, slot * TB : (slot + 1) * TB], p_tp[:])
            if dc < NDC:
                V.tensor_copy(xnT[:, dc * TB : (dc + 1) * TB], p_tp[:])
            m_xn = dve.inc(V.engine_nop())
            T.wait_ge(dve.h, m_xn)
            w = wcat16.rearrange("p (j c) -> p j c", j=NDF)[:, dc, :]
            rhs = xnT16[:, slot * TB : (slot + 1) * TB]
            T.matmul(p_xp1[:], w[:, 0:128], rhs, start=(dc == 0), stop=(dc == NDF - 1))
            T.matmul(p_bc[:], w[:, 128:160], rhs, start=(dc == 0), stop=(dc == NDF - 1))
            mi = T.matmul(p_lg[:], w[:, 160:224], rhs, start=(dc == 0),
                          stop=(dc == NDF - 1))
        m5 = pe.inc(mi)
        V.wait_ge(pe.h, m5)
        V.tensor_copy(xp16[:], p_xp1[:])
        V.tensor_copy(bc32[:], p_bc[:])
        m6 = dve.inc(V.engine_nop())
        A.wait_ge(pe.h, m5)
        m7 = act.inc(A.activation(lgrm[:], p_lg[:], AF.Identity, bias=rb_sb[:, 0:1],
                                  scale=1.0))
        T.wait_ge(act.h, m7)
        T.wait_ge(dve.h, m6)
        for tt in range(NTT):
            ti = T.transpose(p_ms[:, tt * NE : (tt + 1) * NE],
                             lgrm[:, tt * 128 : (tt + 1) * 128], idf[0:NE, 0:NE])
        m8 = pe.inc(ti)
        V.wait_ge(pe.h, m8)
        V.tensor_copy(lgtm[:], p_ms[:, 0 : NTT * NE])
        E_ = lambda tt: etm[:, tt * NE : (tt + 1) * NE]
        L_ = lambda tt: lgtm[:, tt * NE : (tt + 1) * NE]
        M_ = lambda tt: msk[:, tt * NE : (tt + 1) * NE]
        T8 = lambda tt: t8[:, tt * 8 : (tt + 1) * 8]
        W_ = lambda tt: wdtm[:, tt * 72 : tt * 72 + NE + 1]
        for tt in range(NTT):
            V.tensor_reduce(cl(0, tt), L_(tt), AX.X, Op.max)
        for tt in range(NTT):
            V.tensor_scalar(cl(1, tt), cl(0, tt), -1.0, None, Op.mult)
        m9 = dve.inc(V.engine_nop())
        A.wait_ge(dve.h, m9)
        m10 = act.v
        for tt in range(NTT):
            m10 = act.inc(A.activation(E_(tt), L_(tt), AF.Exp,
                                       bias=cl(1, tt), scale=1.0))
        V.wait_ge(act.h, m10)
        for tt in range(NTT):
            V.tensor_reduce(cl(2, tt), E_(tt), AX.X, Op.add)
        for tt in range(NTT):
            V.max(T8(tt), E_(tt))
        for tt in range(NTT):
            V.reciprocal(cl(3, tt), cl(2, tt))
        for tt in range(NTT):
            V.tensor_scalar(M_(tt), E_(tt), t8[:, tt * 8 + 7 : tt * 8 + 8],
                            None, Op.is_ge)
        for tt in range(NTT):
            V.tensor_reduce(cl(4, tt), T8(tt), AX.X, Op.add)
        for tt in range(NTT):
            V.tensor_tensor(wdtm[:, tt * 72 + NE : tt * 72 + NE + 1],
                            cl(4, tt), cl(3, tt), Op.mult)
        for tt in range(NTT):
            V.scalar_tensor_tensor(wdtm[:, tt * 72 : tt * 72 + NE],
                                   E_(tt), cl(3, tt), M_(tt), Op.mult, Op.mult)
        m11 = dve.inc(V.engine_nop())
        T.wait_ge(dve.h, m11)
        for tt in range(NTT):
            ti = T.transpose(p_ms[0 : NE + 1, tt * 128 : tt * 128 + 128],
                             W_(tt), idf[:])
        m12 = pe.inc(ti)
        V.wait_ge(pe.h, m12)
        for tt in range(NTT):
            V.tensor_copy(wd16[:, tt * 128 : (tt + 1) * 128],
                          p_ms[0:NE, tt * 128 : tt * 128 + 128])
        for tt in range(NTT):
            V.tensor_copy(wsrow[0:1, tt * 128 : (tt + 1) * 128],
                          p_ms[NE : NE + 1, tt * 128 : tt * 128 + 128])
        dve.inc(V.engine_nop())
        T.wait_ge(dve.h, dve.v)
        mi = T.matmul(p_bb[:], ones[:], wsrow[:], start=True, stop=True)
        m13 = pe.inc(mi)
        V.wait_ge(pe.h, m13)
        V.tensor_copy(wsb[:], p_bb[:])
        m14 = dve.inc(V.engine_nop())
        m15 = pe.v
        for n in range(32):
            V.wait_ge(pe.h, m15)
            if n > 0:
                V.tensor_copy(arena[:, (n - 1) * TB : n * TB], p_bb[:])
            V.tensor_scalar(sel[:], iopf[:], n, None, Op.is_equal)
            m14 = dve.inc(V.engine_nop())
            T.wait_ge(dve.h, m14)
            mi = T.matmul(p_bb[:], sel[:], bc32[:], start=True, stop=True)
            m15 = pe.inc(mi)
        V.wait_ge(pe.h, m15)
        V.tensor_copy(arena[:, 31 * TB : 32 * TB], p_bb[:])
        dve.inc(V.engine_nop())

        nc.all_engine_barrier()

        m_sp_prev = act.v
        m_u_prev = dve.v
        m_bm_prev = dve.v
        m_st = {}
        m_add = {}
        for dc in range(NDC):
            tcol = slice(dc * TB, (dc + 1) * TB)
            T.wait_ge(act.h, m_sp_prev)
            mi = T.matmul(p_dt[:], dtw16[:, dc * 128 : (dc + 1) * 128], xp16[:],
                          start=True, stop=True)
            m16 = pe.inc(mi)
            A.wait_ge(pe.h, m16)
            A.wait_ge(dve.h, m_u_prev)
            A.activation(sp_st[:], p_dt[:], AF.Exp, bias=bvec[:, dc : dc + 1],
                         scale=1.0)
            A.activation(spare[:], spare[:], AF.Identity, bias=0.0, scale=1.0)
            m_sp_prev = act.inc(A.activation(dt_sb[:], sp_st[:], AF.Ln, bias=1.0,
                                             scale=1.0))
            V.wait_ge(act.h, m_sp_prev)
            V.tensor_tensor(u_sb[:], dt_sb[:], xnT[:, tcol], Op.mult)
            V.scalar_tensor_tensor(y_sb[:], wsb[:], dvec[:, dc : dc + 1],
                                   xnT[:, tcol], Op.add, Op.mult)
            V.tensor_tensor(db_sb[:, 0:TB], u_sb[:], arena[:, 0:TB], Op.mult)
            m_u_prev = dve.inc(V.engine_nop())
            m_scan = {}
            for n in range(N):
                par = n & 1
                dsl = slice(par * TB, (par + 1) * TB)
                if n >= 2:
                    A.wait_ge(dve.h, m_scan[n - 2])
                m_da = act.inc(A.activation(
                    da_sb[:, dsl], dt_sb[:], AF.Exp, bias=0.0,
                    scale=a_sb[:, dc * N + n : dc * N + n + 1]))
                if n + 1 < N:
                    ndsl = slice((1 - par) * TB, (2 - par) * TB)
                    V.tensor_tensor(db_sb[:, ndsl], u_sb[:],
                                    arena[:, (n + 1) * TB : (n + 2) * TB], Op.mult)
                V.wait_ge(act.h, m_da)
                V.tensor_tensor_scan(h_all[:, n * TB : (n + 1) * TB],
                                     da_sb[:, dsl], db_sb[:, dsl],
                                     hst[:, dc * N + n : dc * N + n + 1],
                                     Op.mult, Op.add)
                m_scan[n] = dve.inc(V.engine_nop())
            V.engine_nop()
            V.tensor_copy(hst[:, dc * N : (dc + 1) * N],
                          h_all.rearrange("p (n t) -> p n t", n=N)[:, :, TB - 1])
            V.tensor_tensor(h_all[:], h_all[:], arena[:, 16 * TB : 32 * TB], Op.mult)
            w = 8
            while w >= 1:
                V.tensor_tensor(h_all[:, 0 : w * TB], h_all[:, 0 : w * TB],
                                h_all[:, w * TB : 2 * w * TB], Op.add)
                w //= 2
            V.tensor_tensor(y_sb[:], y_sb[:], h_all[:, 0:TB], Op.add)
            m17 = dve.inc(V.engine_nop())
            T.wait_ge(dve.h, m_bm_prev)
            mi = T.matmul(p_bm[:], eb16[:, dc * 128 : (dc + 1) * 128],
                          wd16[:], start=True, stop=True)
            m18 = pe.inc(mi)
            V.wait_ge(pe.h, m18)
            V.tensor_tensor(y_sb[:], y_sb[:], p_bm[:], Op.add)
            m_bm_prev = dve.inc(V.engine_nop())
            slot = dc & 1
            if (dc - 2) in m_add:
                SY.wait_ge(dve.h, m_add[dc - 2])
            for tt in range(NTT):
                ld.inc(SY.dma_start(
                    xres[:, slot * TB + tt * 128 : slot * TB + (tt + 1) * 128],
                    bass.AP(d_xh, (r0 + tt * 128) * DH + dc * 128,
                            [[DH, 128], [1, 128]])), 16)
            m_ld_res = ld.v
            T.wait_ge(dve.h, m_bm_prev)
            for k in range(4):
                ti = T.transpose(p_tp[:, k * 128 : (k + 1) * 128],
                                 y_sb[:, k * 128 : (k + 1) * 128], idf[:])
            m19 = pe.inc(ti)
            V.wait_ge(pe.h, m19)
            V.wait_ge(ld.h, m_ld_res)
            if (dc - 2) in m_st:
                V.wait_ge(st.h, m_st[dc - 2])
            V.tensor_tensor(outsb[:, slot * TB : (slot + 1) * TB],
                            xres[:, slot * TB : (slot + 1) * TB], p_tp[:], Op.add)
            m20 = dve.inc(V.engine_nop())
            m_add[dc] = m20
            SY.wait_ge(dve.h, m20)
            for tt in range(NTT):
                st.inc(SY.dma_start(
                    bass.AP(d_out, (r0 + tt * 128) * DH + dc * 128,
                            [[DH, 128], [1, 128]]),
                    outsb[:, slot * TB + tt * 128 : slot * TB + (tt + 1) * 128]), 16)
            m_st[dc] = st.v
            T.wait_ge(dve.h, m20)

        nc.all_engine_barrier()

    SY.wait_ge(st.h, st.v)
    SY.wait_ge(ld.h, ld.v)
    nc.all_engine_barrier()
    return nc


def kernel(L0_attn_x, L0_attn_A_log, L0_attn_D, L0_attn_x_proj_w,
           L0_attn_dt_proj_w, L0_attn_dt_proj_b, L0_ffn_router_weight,
           L0_ffn_router_bias, L0_ffn_expert_bias):
    x = np.asarray(L0_attn_x, dtype=np.float32)
    wcat_rows = np.concatenate([np.asarray(L0_attn_x_proj_w, np.float32),
                                np.asarray(L0_ffn_router_weight, np.float32)],
                               axis=0).T
    in_maps = []
    for c in range(8):
        b, dh = c >> 1, c & 1
        sl = slice(dh * DH, (dh + 1) * DH)
        perm = np.r_[dh * DH : (dh + 1) * DH, (1 - dh) * DH : (2 - dh) * DH]
        wc = wcat_rows[perm]
        wc = np.ascontiguousarray(
            wc.reshape(NDF, 128, 224).transpose(1, 0, 2).reshape(128, NDF * 224))
        in_maps.append(dict(
            x=np.ascontiguousarray(x[b][:, perm]),
            xh=np.ascontiguousarray(x[b][:, sl]),
            wcat=wc,
            dtw=np.ascontiguousarray(np.asarray(L0_attn_dt_proj_w, np.float32)[sl].T),
            alog=np.ascontiguousarray(
                np.asarray(L0_attn_A_log, np.float32)[sl]
                .reshape(NDC, 128, N).transpose(1, 0, 2).reshape(128, NDC * N)),
            dvec=np.ascontiguousarray(
                np.asarray(L0_attn_D, np.float32)[sl].reshape(NDC, 128).T),
            bvec=np.ascontiguousarray(
                np.asarray(L0_attn_dt_proj_b, np.float32)[sl].reshape(NDC, 128).T),
            rb=np.ascontiguousarray(
                np.asarray(L0_ffn_router_bias, np.float32)[:, None]),
            eb=np.ascontiguousarray(
                np.asarray(L0_ffn_expert_bias, np.float32)[:, sl]),
        ))
    nc = build_program()
    res = run_bass_kernel_spmd(nc, in_maps, core_ids=list(range(8)))
    out = np.empty((B, S, D), dtype=np.float32)
    for c in range(8):
        b, dh = c >> 1, c & 1
        out[b, :, dh * DH : (dh + 1) * DH] = res.results[c]["out"]
    return out


# version 05
# speedup vs baseline: 1.0148x; 1.0148x over previous
import numpy as np
import concourse.bass as bass
import concourse.mybir as mybir
from concourse.bass_utils import run_bass_kernel_spmd
from concourse.alu_op_type import AluOpType as Op

F32 = mybir.dt.float32
BF16 = mybir.dt.bfloat16
I32 = mybir.dt.int32
AF = mybir.ActivationFunctionType
AX = mybir.AxisListType

B, S, D, N, RNK, NE = 4, 2048, 4096, 16, 128, 64
DH = D // 2
TB = 512
NT = S // TB
NDC = DH // 128
NDF = D // 128
NTT = TB // 128
USE_GPSIMD = True
NT_LIM = NT
DC_LIM = NDC
A_STAGE = 6
NOACC = False


class Ctr:
    def __init__(self, nc, name):
        self.h = nc.alloc_semaphore(name)
        self.v = 0

    def inc(self, instr, k=1):
        instr.then_inc(self.h, k)
        self.v += k
        return self.v


def build_program():
    nc = bass.Bass("TRN2", target_bir_lowering=False)

    d_x = nc.dram_tensor("x", [S, D], F32, kind="ExternalInput")
    d_xh = nc.dram_tensor("xh", [S, DH], F32, kind="ExternalInput")
    d_wcat = nc.dram_tensor("wcat", [128, NDF * 224], F32, kind="ExternalInput")
    d_dtw = nc.dram_tensor("dtw", [RNK, DH], F32, kind="ExternalInput")
    d_alog = nc.dram_tensor("alog", [128, NDC * N], F32, kind="ExternalInput")
    d_dvec = nc.dram_tensor("dvec", [128, NDC], F32, kind="ExternalInput")
    d_bvec = nc.dram_tensor("bvec", [128, NDC], F32, kind="ExternalInput")
    d_rb = nc.dram_tensor("rb", [NE, 1], F32, kind="ExternalInput")
    d_eb = nc.dram_tensor("eb", [NE, DH], F32, kind="ExternalInput")
    d_out = nc.dram_tensor("out", [S, DH], F32, kind="ExternalOutput")

    arena = nc.alloc_sbuf_tensor("arena", [128, NTT * D], F32)
    xnT = nc.alloc_sbuf_tensor("xnT", [128, NDC * TB], F32)
    xnT16 = nc.alloc_sbuf_tensor("xnT16", [128, 2 * TB], BF16)
    h_all = nc.alloc_sbuf_tensor("h_all", [128, N * TB], F32)
    wcat16 = nc.alloc_sbuf_tensor("wcat16", [128, NDF * 224], BF16)
    dtw16 = nc.alloc_sbuf_tensor("dtw16", [128, DH], BF16)
    eb16 = nc.alloc_sbuf_tensor("eb16", [NE, DH], BF16)
    a_sb = nc.alloc_sbuf_tensor("a_sb", [128, NDC * N], F32)
    dvec = nc.alloc_sbuf_tensor("dvec_sb", [128, NDC], F32)
    bvec = nc.alloc_sbuf_tensor("bvec_sb", [128, NDC], F32)
    rb_sb = nc.alloc_sbuf_tensor("rb_sb", [NE, 1], F32)
    wd16 = nc.alloc_sbuf_tensor("wd16", [NE, TB], BF16)
    xp16 = nc.alloc_sbuf_tensor("xp16", [128, TB], BF16)
    bc32 = nc.alloc_sbuf_tensor("bc32", [32, TB], F32)
    lgrm = nc.alloc_sbuf_tensor("lgrm", [NE, TB], F32)
    lgtm = nc.alloc_sbuf_tensor("lgtm", [128, NTT * NE], F32)
    etm = nc.alloc_sbuf_tensor("etm", [128, NTT * NE], F32)
    msk = nc.alloc_sbuf_tensor("msk", [128, NTT * NE], F32)
    wdtm = nc.alloc_sbuf_tensor("wdtm", [128, NTT * 72], F32)
    t8 = nc.alloc_sbuf_tensor("t8", [128, NTT * 8], F32)
    cols = nc.alloc_sbuf_tensor("cols", [128, 48], F32)
    spare = nc.alloc_sbuf_tensor("spare", [128, 1], F32)
    wsrow = nc.alloc_sbuf_tensor("wsrow", [1, TB], F32)
    wsb = nc.alloc_sbuf_tensor("wsb", [128, TB], F32)
    dt_sb = nc.alloc_sbuf_tensor("dt_sb", [128, 2 * TB], F32)
    sp_st = nc.alloc_sbuf_tensor("sp_st", [128, 2 * TB], F32)
    u_sb = nc.alloc_sbuf_tensor("u_sb", [128, 2 * TB], F32)
    y_sb = nc.alloc_sbuf_tensor("y_sb", [128, 2 * TB], F32)
    da_sb = nc.alloc_sbuf_tensor("da_sb", [128, 2 * TB], F32)
    db_sb = nc.alloc_sbuf_tensor("db_sb", [128, 4 * TB], F32)
    hst = nc.alloc_sbuf_tensor("hst", [128, NDC * N], F32)
    xres = nc.alloc_sbuf_tensor("xres", [128, 2 * TB], F32)
    outsb = nc.alloc_sbuf_tensor("outsb", [128, 2 * TB], F32)
    idf = nc.alloc_sbuf_tensor("idf", [128, 128], F32)
    idi = nc.alloc_sbuf_tensor("idi", [128, 128], I32)
    iopf = nc.alloc_sbuf_tensor("iopf", [32, 128], I32)
    sel = nc.alloc_sbuf_tensor("sel", [32, 2 * 128], F32)
    ones = nc.alloc_sbuf_tensor("ones", [1, 128], F32)
    eps_sb = nc.alloc_sbuf_tensor("eps_sb", [128, 1], F32)

    p0 = nc.alloc_psum_tensor("p0", [128, 512], F32)
    p1 = nc.alloc_psum_tensor("p1", [128, 512], F32)
    p2 = nc.alloc_psum_tensor("p2", [128, 512], F32)
    p3 = nc.alloc_psum_tensor("p3", [128, 512], F32)
    p4 = nc.alloc_psum_tensor("p4", [128, 512], F32)
    p5 = nc.alloc_psum_tensor("p5", [128, 512], F32)
    p6 = nc.alloc_psum_tensor("p6", [128, 512], F32)
    p7 = nc.alloc_psum_tensor("p7", [128, 512], F32)

    ld = Ctr(nc, "c_ld")
    pe = Ctr(nc, "c_pe")
    act = Ctr(nc, "c_act")
    dve = Ctr(nc, "c_dve")
    st = Ctr(nc, "c_st")
    gp = Ctr(nc, "c_gp")

    nc.all_engine_barrier()
    nc.gpsimd.sem_clear(range(ld.h.num, gp.h.num + 1))
    nc.all_engine_barrier()

    V, A, T, G, SY = nc.vector, nc.scalar, nc.tensor, nc.gpsimd, nc.sync

    ld.inc(SY.dma_start(arena[:, 0 : NDF * 224], d_wcat[:, :]), 16)
    ld.inc(SY.dma_start(h_all[:, 0:DH], d_dtw[:, :]), 16)
    ld.inc(SY.dma_start(h_all[0:NE, DH : 2 * DH], d_eb[:, :]), 16)
    ld.inc(SY.dma_start(cols[0:NE, 0:1], d_rb[:, :]), 16)
    ld.inc(SY.dma_start(a_sb[:, :], d_alog[:, :]), 16)
    ld.inc(SY.dma_start(dvec[:, :], d_dvec[:, :]), 16)
    ld.inc(SY.dma_start(bvec[:, :], d_bvec[:, :]), 16)

    G.iota(idi[:], pattern=[[1, 128]], base=0, channel_multiplier=-1)
    gp.inc(G.iota(iopf[:], pattern=[[0, 128]], base=0, channel_multiplier=1))

    V.wait_ge(ld.h, ld.v)
    V.wait_ge(gp.h, gp.v)
    V.tensor_scalar(idf[:], idi[:], 0, None, Op.is_equal)
    V.memset(ones[:], 1.0)
    V.memset(eps_sb[:], 1e-5)
    V.memset(spare[:], 0.0)
    V.memset(hst[:], 0.0)
    V.tensor_copy(wcat16[:], arena[:, 0 : NDF * 224])
    V.tensor_copy(dtw16[:], h_all[:, 0:DH])
    V.tensor_copy(eb16[:], h_all[0:NE, DH : 2 * DH])
    V.tensor_copy(rb_sb[:], cols[0:NE, 0:1])
    mv = dve.inc(V.engine_nop())
    A.wait_ge(dve.h, mv)
    ma = act.inc(A.activation(a_sb[:], a_sb[:], AF.Exp, bias=0.0, scale=1.0))
    V.wait_ge(act.h, ma)
    V.tensor_scalar(a_sb[:], a_sb[:], -1.0, None, Op.mult)
    dve.inc(V.engine_nop())

    nc.all_engine_barrier()

    m_gp = {}
    m_scan = {}
    m_sp = {}
    m_u = {}
    m_ydone = {}
    m_bm = {}
    m_dtmm = {}
    m_ytr = {}
    m_oadd = {}
    m_st = {}
    for tau in range(NT_LIM):
        r0 = tau * TB
        for tt in range(NTT):
            ld.inc(SY.dma_start(arena[:, tt * D : (tt + 1) * D],
                                d_x[r0 + tt * 128 : r0 + (tt + 1) * 128, :]), 16)
        m_ld_x = ld.v
        V.wait_ge(ld.h, m_ld_x)
        if A_STAGE < 2:
            nc.all_engine_barrier()
            continue
        XT = [arena[:, tt * D : (tt + 1) * D] for tt in range(NTT)]
        cl = lambda j, tt: cols[:, 4 * j + tt : 4 * j + tt + 1]
        for tt in range(NTT):
            V.tensor_reduce(cl(0, tt), XT[tt], AX.X, Op.add)
        for tt in range(NTT):
            if NOACC:
                V.tensor_reduce(cl(1, tt), XT[tt], AX.X, Op.add)
            else:
                V.scalar_tensor_tensor(h_all[:, (tt & 1) * D : (tt & 1) * D + D],
                                       XT[tt], 1.0, XT[tt],
                                       Op.mult, Op.mult, accum_out=cl(1, tt))
        for tt in range(NTT):
            V.tensor_scalar(cl(2, tt), cl(0, tt), 1.0 / D, None, Op.mult)
        for tt in range(NTT):
            V.tensor_scalar(cl(3, tt), cl(1, tt), 1.0 / D, None, Op.mult)
        for tt in range(NTT):
            V.scalar_tensor_tensor(cl(4, tt), cl(2, tt), 1.0, cl(2, tt),
                                   Op.mult, Op.mult)
        for tt in range(NTT):
            V.tensor_tensor(cl(5, tt), cl(3, tt), cl(4, tt), Op.subtract)
        m1 = dve.inc(V.engine_nop())
        if A_STAGE < 3:
            nc.all_engine_barrier()
            continue
        A.wait_ge(dve.h, m1)
        for tt in range(NTT):
            A.activation(cl(6, tt), cl(5, tt), AF.Ln, bias=eps_sb[:, 0:1],
                         scale=1.0)
        m2 = act.v
        for tt in range(NTT):
            m2 = act.inc(A.activation(cl(7, tt), cl(6, tt), AF.Exp, bias=0.0,
                                      scale=-0.5))
        V.wait_ge(act.h, m2)
        for tt in range(NTT):
            V.scalar_tensor_tensor(cl(8, tt), cl(2, tt), -1.0, cl(7, tt),
                                   Op.mult, Op.mult)
        for tt in range(NTT):
            V.tensor_scalar(XT[tt], XT[tt], cl(7, tt), cl(8, tt), Op.mult, Op.add)
        m_xn = dve.inc(V.engine_nop())
        if A_STAGE < 4:
            nc.all_engine_barrier()
            continue

        T.wait_ge(dve.h, m_xn)
        m_ev16 = {}
        m_evf = {}
        for dc in range(NDF):
            slot = dc & 1
            ptp = (p0, p1)[slot]
            if dc >= 2:
                T.wait_ge(act.h, m_ev16[dc - 2])
                T.wait_ge(dve.h, m_evf[dc - 2])
            for tt in range(NTT):
                ti = T.transpose(ptp[:, tt * 128 : (tt + 1) * 128],
                                 arena[:, tt * D + dc * 128 : tt * D + (dc + 1) * 128],
                                 idf[:])
            m4 = pe.inc(ti)
            A.wait_ge(pe.h, m4)
            m_ev16[dc] = act.inc(A.activation(
                xnT16[:, slot * TB : (slot + 1) * TB], ptp[:], AF.Copy,
                bias=0.0, scale=1.0))
            if dc < NDC:
                V.wait_ge(act.h, m_ev16[dc])
                V.tensor_copy(xnT[:, dc * TB : (dc + 1) * TB], ptp[:])
                m_evf[dc] = dve.inc(V.engine_nop())
            else:
                m_evf[dc] = m_evf[dc - 1]
            T.wait_ge(act.h, m_ev16[dc])
            w = wcat16.rearrange("p (j c) -> p j c", j=NDF)[:, dc, :]
            rhs = xnT16[:, slot * TB : (slot + 1) * TB]
            T.matmul(p2[:], w[:, 0:128], rhs, start=(dc == 0), stop=(dc == NDF - 1))
            T.matmul(p3[0:32, :], w[:, 128:160], rhs, start=(dc == 0),
                     stop=(dc == NDF - 1))
            mi = T.matmul(p4[0:NE, :], w[:, 160:224], rhs, start=(dc == 0),
                          stop=(dc == NDF - 1))
        m5 = pe.inc(mi)
        A.wait_ge(pe.h, m5)
        A.activation(xp16[:], p2[:], AF.Copy, bias=0.0, scale=1.0)
        A.activation(bc32[:], p3[0:32, :], AF.Copy, bias=0.0, scale=1.0)
        m7 = act.inc(A.activation(lgrm[:], p4[0:NE, :], AF.Identity,
                                  bias=rb_sb[:, 0:1], scale=1.0))
        if A_STAGE < 5:
            nc.all_engine_barrier()
            continue
        T.wait_ge(act.h, m7)
        for tt in range(NTT):
            ti = T.transpose(p5[:, tt * NE : (tt + 1) * NE],
                             lgrm[:, tt * 128 : (tt + 1) * 128], idf[0:NE, 0:NE])
        m8 = pe.inc(ti)
        V.wait_ge(pe.h, m8)
        V.tensor_copy(lgtm[:], p5[:, 0 : NTT * NE])
        E_ = lambda tt: etm[:, tt * NE : (tt + 1) * NE]
        L_ = lambda tt: lgtm[:, tt * NE : (tt + 1) * NE]
        M_ = lambda tt: msk[:, tt * NE : (tt + 1) * NE]
        T8 = lambda tt: t8[:, tt * 8 : (tt + 1) * 8]
        W_ = lambda tt: wdtm[:, tt * 72 : tt * 72 + NE + 1]
        for tt in range(NTT):
            V.tensor_reduce(cl(0, tt), L_(tt), AX.X, Op.max)
        for tt in range(NTT):
            V.tensor_scalar(cl(1, tt), cl(0, tt), -1.0, None, Op.mult)
        m9 = dve.inc(V.engine_nop())
        A.wait_ge(dve.h, m9)
        m10 = act.v
        for tt in range(NTT):
            m10 = act.inc(A.activation(E_(tt), L_(tt), AF.Exp,
                                       bias=cl(1, tt), scale=1.0))
        V.wait_ge(act.h, m10)
        for tt in range(NTT):
            V.tensor_reduce(cl(2, tt), E_(tt), AX.X, Op.add)
        for tt in range(NTT):
            V.max(T8(tt), E_(tt))
        for tt in range(NTT):
            V.reciprocal(cl(3, tt), cl(2, tt))
        for tt in range(NTT):
            V.tensor_scalar(M_(tt), E_(tt), t8[:, tt * 8 + 7 : tt * 8 + 8],
                            None, Op.is_ge)
        for tt in range(NTT):
            V.tensor_reduce(cl(4, tt), T8(tt), AX.X, Op.add)
        for tt in range(NTT):
            V.tensor_tensor(wdtm[:, tt * 72 + NE : tt * 72 + NE + 1],
                            cl(4, tt), cl(3, tt), Op.mult)
        for tt in range(NTT):
            V.scalar_tensor_tensor(wdtm[:, tt * 72 : tt * 72 + NE],
                                   E_(tt), cl(3, tt), M_(tt), Op.mult, Op.mult)
        m11 = dve.inc(V.engine_nop())
        T.wait_ge(dve.h, m11)
        for tt in range(NTT):
            ti = T.transpose(p5[0 : NE + 1, tt * 128 : tt * 128 + 128],
                             W_(tt), idf[:])
        m12 = pe.inc(ti)
        A.wait_ge(pe.h, m12)
        for tt in range(NTT):
            A.activation(wd16[:, tt * 128 : (tt + 1) * 128],
                         p5[0:NE, tt * 128 : tt * 128 + 128], AF.Copy,
                         bias=0.0, scale=1.0)
        m12b = act.v
        for tt in range(NTT):
            m12b = act.inc(A.activation(wsrow[0:1, tt * 128 : (tt + 1) * 128],
                                        p5[NE : NE + 1, tt * 128 : tt * 128 + 128],
                                        AF.Copy, bias=0.0, scale=1.0))
        if A_STAGE < 6:
            nc.all_engine_barrier()
            continue
        T.wait_ge(act.h, m12b)
        T.wait_ge(dve.h, m_evf[NDC - 1])
        mi = T.matmul(p6[:], ones[:], wsrow[:], start=True, stop=True)
        m13 = pe.inc(mi)
        A.wait_ge(pe.h, m13)
        m_ws = act.inc(A.activation(wsb[:], p6[:], AF.Copy, bias=0.0, scale=1.0))
        m_bc_arena = {}
        m_bc_mm = {}
        for n in range(32):
            pbb = (p6, p7)[n & 1]
            if n >= 2:
                V.wait_ge(pe.h, m_bc_mm[n - 2])
            V.tensor_scalar(sel[:, (n & 1) * 128 : (n & 1) * 128 + 128],
                            iopf[:], n, None, Op.is_equal)
            msel = dve.inc(V.engine_nop())
            T.wait_ge(dve.h, msel)
            if n == 0:
                T.wait_ge(act.h, m_ws)
            if n >= 2:
                T.wait_ge(act.h, m_bc_arena[n - 2])
            mi = T.matmul(pbb[:], sel[:, (n & 1) * 128 : (n & 1) * 128 + 128],
                          bc32[:], start=True, stop=True)
            mm = m_bc_mm[n] = pe.inc(mi)
            A.wait_ge(pe.h, mm)
            m_bc_arena[n] = act.inc(A.activation(
                arena[:, n * TB : (n + 1) * TB], pbb[:], AF.Copy,
                bias=0.0, scale=1.0))

        nc.all_engine_barrier()

        for dc in range(DC_LIM):
            g = tau * NDC + dc
            sl2 = dc & 1
            pdt = (p0, p1)[sl2]
            pbm = (p2, p3)[sl2]
            ptpy = (p4, p5)[sl2]
            tcol = slice(dc * TB, (dc + 1) * TB)
            dsl = slice(sl2 * TB, (sl2 + 1) * TB)

            if g >= 2:
                T.wait_ge(act.h, m_sp[g - 2])
            mi = T.matmul(pdt[:], dtw16[:, dc * 128 : (dc + 1) * 128], xp16[:],
                          start=True, stop=True)
            m_dtmm[g] = pe.inc(mi)
            if g >= 2:
                T.wait_ge(dve.h, m_ydone[g - 2])
            mi = T.matmul(pbm[:], eb16[:, dc * 128 : (dc + 1) * 128], wd16[:],
                          start=True, stop=True)
            m_bm[g] = pe.inc(mi)

            A.wait_ge(pe.h, m_dtmm[g])
            if g >= 2:
                A.wait_ge(dve.h, m_u[g - 2])
            A.activation(sp_st[:, dsl], pdt[:], AF.Exp,
                         bias=bvec[:, dc : dc + 1], scale=1.0)
            A.activation(spare[:], spare[:], AF.Identity, bias=0.0, scale=1.0)
            m_sp[g] = act.inc(A.activation(dt_sb[:, dsl], sp_st[:, dsl],
                                            AF.Ln, bias=1.0, scale=1.0))

            V.wait_ge(act.h, m_sp[g])
            if g >= 2:
                V.wait_ge(pe.h, m_ytr.get(g - 2, 0))
                if USE_GPSIMD:
                    V.wait_ge(gp.h, m_gp[(g - 2, N - 1)])
            V.tensor_tensor(u_sb[:, dsl], dt_sb[:, dsl], xnT[:, tcol], Op.mult)
            V.scalar_tensor_tensor(y_sb[:, dsl], wsb[:], dvec[:, dc : dc + 1],
                                   xnT[:, tcol], Op.add, Op.mult)
            V.tensor_tensor(db_sb[:, 0:TB], u_sb[:, dsl], arena[:, 0:TB], Op.mult)
            m_u[g] = dve.inc(V.engine_nop())

            if USE_GPSIMD:
                G.wait_ge(dve.h, m_u[g])
                for k in (1, 3):
                    if (g - 1, k + 12) in m_scan:
                        G.wait_ge(dve.h, m_scan[(g - 1, k + 12)])
                    gi = G.tensor_tensor(db_sb[:, (k & 3) * TB : (k & 3) * TB + TB],
                                         u_sb[:, dsl],
                                         arena[:, k * TB : (k + 1) * TB], Op.mult)
                    m_gp[(g, k)] = gp.inc(gi)

            for n in range(N):
                par = n & 1
                adsl = slice(par * TB, (par + 1) * TB)
                if n >= 2:
                    A.wait_ge(dve.h, m_scan[(g, n - 2)])
                else:
                    A.wait_ge(dve.h, m_u[g])
                m_da = act.inc(A.activation(
                    da_sb[:, adsl], dt_sb[:, dsl], AF.Exp, bias=0.0,
                    scale=a_sb[:, dc * N + n : dc * N + n + 1]))
                k = n + 2
                if k < N:
                    if (k & 1) == 0 or not USE_GPSIMD:
                        V.tensor_tensor(
                            db_sb[:, (k & 3) * TB : (k & 3) * TB + TB],
                            u_sb[:, dsl], arena[:, k * TB : (k + 1) * TB],
                            Op.mult)
                    elif k >= 5:
                        G.wait_ge(dve.h, m_scan[(g, k - 4)])
                        gi = G.tensor_tensor(
                            db_sb[:, (k & 3) * TB : (k & 3) * TB + TB],
                            u_sb[:, dsl], arena[:, k * TB : (k + 1) * TB],
                            Op.mult)
                        m_gp[(g, k)] = gp.inc(gi)
                if not USE_GPSIMD and n == 1:
                    V.tensor_tensor(db_sb[:, (n & 3) * TB : (n & 3) * TB + TB],
                                    u_sb[:, dsl],
                                    arena[:, n * TB : (n + 1) * TB], Op.mult)
                V.wait_ge(act.h, m_da)
                if par == 1 and USE_GPSIMD:
                    V.wait_ge(gp.h, m_gp[(g, n)])
                V.tensor_tensor_scan(h_all[:, n * TB : (n + 1) * TB],
                                     da_sb[:, adsl],
                                     db_sb[:, (n & 3) * TB : (n & 3) * TB + TB],
                                     hst[:, dc * N + n : dc * N + n + 1],
                                     Op.mult, Op.add)
                m_scan[(g, n)] = dve.inc(V.engine_nop())

            V.engine_nop()
            V.tensor_copy(hst[:, dc * N : (dc + 1) * N],
                          h_all.rearrange("p (n t) -> p n t", n=N)[:, :, TB - 1])
            V.tensor_tensor(h_all[:], h_all[:], arena[:, 16 * TB : 32 * TB],
                            Op.mult)
            w = 8
            while w >= 1:
                V.tensor_tensor(h_all[:, 0 : w * TB], h_all[:, 0 : w * TB],
                                h_all[:, w * TB : 2 * w * TB], Op.add)
                w //= 2
            V.tensor_tensor(y_sb[:, dsl], y_sb[:, dsl], h_all[:, 0:TB], Op.add)
            V.wait_ge(pe.h, m_bm[g])
            V.tensor_tensor(y_sb[:, dsl], y_sb[:, dsl], pbm[:], Op.add)
            m_ydone[g] = dve.inc(V.engine_nop())

            if (g - 2) in m_oadd:
                SY.wait_ge(dve.h, m_oadd[g - 2])
            for tt in range(NTT):
                ld.inc(SY.dma_start(
                    xres[:, sl2 * TB + tt * 128 : sl2 * TB + (tt + 1) * 128],
                    bass.AP(d_xh, (r0 + tt * 128) * DH + dc * 128,
                            [[DH, 128], [1, 128]])), 16)
            m_ld_res = ld.v

            T.wait_ge(dve.h, m_ydone[g])
            if (g - 2) in m_oadd:
                T.wait_ge(dve.h, m_oadd[g - 2])
            for k in range(4):
                ti = T.transpose(
                    ptpy[:, k * 128 : (k + 1) * 128],
                    y_sb[:, sl2 * TB + k * 128 : sl2 * TB + (k + 1) * 128],
                    idf[:])
            m_ytr[g] = pe.inc(ti)
            V.wait_ge(pe.h, m_ytr[g])
            V.wait_ge(ld.h, m_ld_res)
            if (g - 2) in m_st:
                V.wait_ge(st.h, m_st[g - 2])
            V.tensor_tensor(outsb[:, dsl], xres[:, dsl], ptpy[:], Op.add)
            m_oadd[g] = dve.inc(V.engine_nop())
            SY.wait_ge(dve.h, m_oadd[g])
            for tt in range(NTT):
                st.inc(SY.dma_start(
                    bass.AP(d_out, (r0 + tt * 128) * DH + dc * 128,
                            [[DH, 128], [1, 128]]),
                    outsb[:, sl2 * TB + tt * 128 : sl2 * TB + (tt + 1) * 128]), 16)
            m_st[g] = st.v

        nc.all_engine_barrier()

    SY.wait_ge(st.h, st.v)
    SY.wait_ge(ld.h, ld.v)
    nc.all_engine_barrier()
    return nc


def kernel(L0_attn_x, L0_attn_A_log, L0_attn_D, L0_attn_x_proj_w,
           L0_attn_dt_proj_w, L0_attn_dt_proj_b, L0_ffn_router_weight,
           L0_ffn_router_bias, L0_ffn_expert_bias):
    x = np.asarray(L0_attn_x, dtype=np.float32)
    wcat_rows = np.concatenate([np.asarray(L0_attn_x_proj_w, np.float32),
                                np.asarray(L0_ffn_router_weight, np.float32)],
                               axis=0).T
    in_maps = []
    for c in range(8):
        b, dh = c >> 1, c & 1
        sl = slice(dh * DH, (dh + 1) * DH)
        perm = np.r_[dh * DH : (dh + 1) * DH, (1 - dh) * DH : (2 - dh) * DH]
        wc = wcat_rows[perm]
        wc = np.ascontiguousarray(
            wc.reshape(NDF, 128, 224).transpose(1, 0, 2).reshape(128, NDF * 224))
        in_maps.append(dict(
            x=np.ascontiguousarray(x[b][:, perm]),
            xh=np.ascontiguousarray(x[b][:, sl]),
            wcat=wc,
            dtw=np.ascontiguousarray(np.asarray(L0_attn_dt_proj_w, np.float32)[sl].T),
            alog=np.ascontiguousarray(
                np.asarray(L0_attn_A_log, np.float32)[sl]
                .reshape(NDC, 128, N).transpose(1, 0, 2).reshape(128, NDC * N)),
            dvec=np.ascontiguousarray(
                np.asarray(L0_attn_D, np.float32)[sl].reshape(NDC, 128).T),
            bvec=np.ascontiguousarray(
                np.asarray(L0_attn_dt_proj_b, np.float32)[sl].reshape(NDC, 128).T),
            rb=np.ascontiguousarray(
                np.asarray(L0_ffn_router_bias, np.float32)[:, None]),
            eb=np.ascontiguousarray(
                np.asarray(L0_ffn_expert_bias, np.float32)[:, sl]),
        ))
    nc = build_program()
    res = run_bass_kernel_spmd(nc, in_maps, core_ids=list(range(8)))
    out = np.empty((B, S, D), dtype=np.float32)
    for c in range(8):
        b, dh = c >> 1, c & 1
        out[b, :, dh * DH : (dh + 1) * DH] = res.results[c]["out"]
    return out


# version 06
# speedup vs baseline: 1.1344x; 1.1179x over previous
import numpy as np
import concourse.bass as bass
import concourse.mybir as mybir
from concourse.bass_utils import run_bass_kernel_spmd
from concourse.alu_op_type import AluOpType as Op

F32 = mybir.dt.float32
BF16 = mybir.dt.bfloat16
I32 = mybir.dt.int32
AF = mybir.ActivationFunctionType
AX = mybir.AxisListType

B, S, D, N, RNK, NE = 4, 2048, 4096, 16, 128, 64
DH = D // 2
TB = 512
NT = S // TB
NDC = DH // 128
NDF = D // 128
NTT = TB // 128
USE_GPSIMD = True
NT_LIM = NT
DC_LIM = NDC
A_STAGE = 6
NOACC = False


class Ctr:
    def __init__(self, nc, name):
        self.h = nc.alloc_semaphore(name)
        self.v = 0

    def inc(self, instr, k=1):
        instr.then_inc(self.h, k)
        self.v += k
        return self.v


def build_program():
    nc = bass.Bass("TRN2", target_bir_lowering=False)

    d_x = nc.dram_tensor("x", [S, D], F32, kind="ExternalInput")
    d_xh = nc.dram_tensor("xh", [S, DH], F32, kind="ExternalInput")
    d_wcat = nc.dram_tensor("wcat", [128, NDF * 224], F32, kind="ExternalInput")
    d_dtw = nc.dram_tensor("dtw", [RNK, DH], F32, kind="ExternalInput")
    d_alog = nc.dram_tensor("alog", [128, NDC * N], F32, kind="ExternalInput")
    d_dvec = nc.dram_tensor("dvec", [128, NDC], F32, kind="ExternalInput")
    d_bvec = nc.dram_tensor("bvec", [128, NDC], F32, kind="ExternalInput")
    d_rb = nc.dram_tensor("rb", [NE, 1], F32, kind="ExternalInput")
    d_eb = nc.dram_tensor("eb", [NE, DH], F32, kind="ExternalInput")
    d_out = nc.dram_tensor("out", [S, DH], F32, kind="ExternalOutput")

    arena = nc.alloc_sbuf_tensor("arena", [128, NTT * D], F32)
    xnT = nc.alloc_sbuf_tensor("xnT", [128, NDC * TB], F32)
    xnT16 = nc.alloc_sbuf_tensor("xnT16", [128, 2 * TB], BF16)
    h_all = nc.alloc_sbuf_tensor("h_all", [128, N * TB], F32)
    wcat16 = nc.alloc_sbuf_tensor("wcat16", [128, NDF * 224], BF16)
    dtw16 = nc.alloc_sbuf_tensor("dtw16", [128, DH], BF16)
    eb16 = nc.alloc_sbuf_tensor("eb16", [NE, DH], BF16)
    a_sb = nc.alloc_sbuf_tensor("a_sb", [128, NDC * N], F32)
    dvec = nc.alloc_sbuf_tensor("dvec_sb", [128, NDC], F32)
    bvec = nc.alloc_sbuf_tensor("bvec_sb", [128, NDC], F32)
    rb_sb = nc.alloc_sbuf_tensor("rb_sb", [NE, 1], F32)
    wd16 = nc.alloc_sbuf_tensor("wd16", [NE, TB], BF16)
    xp16 = nc.alloc_sbuf_tensor("xp16", [128, TB], BF16)
    bc32 = nc.alloc_sbuf_tensor("bc32", [32, TB], F32)
    lgrm = nc.alloc_sbuf_tensor("lgrm", [NE, TB], F32)
    lgtm = nc.alloc_sbuf_tensor("lgtm", [128, NTT * NE], F32)
    etm = nc.alloc_sbuf_tensor("etm", [128, NTT * NE], F32)
    msk = nc.alloc_sbuf_tensor("msk", [128, NTT * NE], F32)
    wdtm = nc.alloc_sbuf_tensor("wdtm", [128, NTT * 72], F32)
    t8 = nc.alloc_sbuf_tensor("t8", [128, NTT * 8], F32)
    cols = nc.alloc_sbuf_tensor("cols", [128, 48], F32)
    spare = nc.alloc_sbuf_tensor("spare", [128, 1], F32)
    wsrow = nc.alloc_sbuf_tensor("wsrow", [1, TB], F32)
    wsb = nc.alloc_sbuf_tensor("wsb", [128, TB], F32)
    dt_sb = nc.alloc_sbuf_tensor("dt_sb", [128, 2 * TB], F32)
    sp_st = nc.alloc_sbuf_tensor("sp_st", [128, 2 * TB], F32)
    u_sb = nc.alloc_sbuf_tensor("u_sb", [128, 2 * TB], F32)
    y_sb = nc.alloc_sbuf_tensor("y_sb", [128, 2 * TB], F32)
    da_sb = nc.alloc_sbuf_tensor("da_sb", [128, 2 * TB], F32)
    db_sb = nc.alloc_sbuf_tensor("db_sb", [128, 4 * TB], F32)
    hst = nc.alloc_sbuf_tensor("hst", [128, NDC * N], F32)
    xres = nc.alloc_sbuf_tensor("xres", [128, 2 * TB], F32)
    outsb = nc.alloc_sbuf_tensor("outsb", [128, 2 * TB], F32)
    idf = nc.alloc_sbuf_tensor("idf", [128, 128], F32)
    idi = nc.alloc_sbuf_tensor("idi", [128, 128], I32)
    iopf = nc.alloc_sbuf_tensor("iopf", [32, 128], I32)
    sel = nc.alloc_sbuf_tensor("sel", [32, 2 * 128], F32)
    ones = nc.alloc_sbuf_tensor("ones", [1, 128], F32)
    eps_sb = nc.alloc_sbuf_tensor("eps_sb", [128, 1], F32)

    p0 = nc.alloc_psum_tensor("p0", [128, 512], F32)
    p1 = nc.alloc_psum_tensor("p1", [128, 512], F32)
    p2 = nc.alloc_psum_tensor("p2", [128, 512], F32)
    p3 = nc.alloc_psum_tensor("p3", [128, 512], F32)
    p4 = nc.alloc_psum_tensor("p4", [128, 512], F32)
    p5 = nc.alloc_psum_tensor("p5", [128, 512], F32)
    p6 = nc.alloc_psum_tensor("p6", [128, 512], F32)
    p7 = nc.alloc_psum_tensor("p7", [128, 512], F32)

    ld = Ctr(nc, "c_ld")
    pe = Ctr(nc, "c_pe")
    act = Ctr(nc, "c_act")
    dve = Ctr(nc, "c_dve")
    st = Ctr(nc, "c_st")
    gp = Ctr(nc, "c_gp")

    nc.all_engine_barrier()
    nc.gpsimd.sem_clear(range(ld.h.num, gp.h.num + 1))
    nc.all_engine_barrier()

    V, A, T, G, SY = nc.vector, nc.scalar, nc.tensor, nc.gpsimd, nc.sync

    ld.inc(SY.dma_start(arena[:, 0 : NDF * 224], d_wcat[:, :]), 16)
    ld.inc(SY.dma_start(h_all[:, 0:DH], d_dtw[:, :]), 16)
    ld.inc(SY.dma_start(h_all[0:NE, DH : 2 * DH], d_eb[:, :]), 16)
    ld.inc(SY.dma_start(cols[0:NE, 0:1], d_rb[:, :]), 16)
    ld.inc(SY.dma_start(a_sb[:, :], d_alog[:, :]), 16)
    ld.inc(SY.dma_start(dvec[:, :], d_dvec[:, :]), 16)
    ld.inc(SY.dma_start(bvec[:, :], d_bvec[:, :]), 16)

    G.iota(idi[:], pattern=[[1, 128]], base=0, channel_multiplier=-1)
    gp.inc(G.iota(iopf[:], pattern=[[0, 128]], base=0, channel_multiplier=1))

    V.wait_ge(ld.h, ld.v)
    V.wait_ge(gp.h, gp.v)
    V.tensor_scalar(idf[:], idi[:], 0, None, Op.is_equal)
    V.memset(ones[:], 1.0)
    V.memset(eps_sb[:], 1e-5)
    V.memset(spare[:], 0.0)
    V.memset(hst[:], 0.0)
    V.tensor_copy(wcat16[:], arena[:, 0 : NDF * 224])
    V.tensor_copy(dtw16[:], h_all[:, 0:DH])
    V.tensor_copy(eb16[:], h_all[0:NE, DH : 2 * DH])
    V.tensor_copy(rb_sb[:], cols[0:NE, 0:1])
    mv = dve.inc(V.engine_nop())
    A.wait_ge(dve.h, mv)
    ma = act.inc(A.activation(a_sb[:], a_sb[:], AF.Exp, bias=0.0, scale=1.0))
    V.wait_ge(act.h, ma)
    V.tensor_scalar(a_sb[:], a_sb[:], -1.0, None, Op.mult)
    dve.inc(V.engine_nop())

    nc.all_engine_barrier()

    m_gp = {}
    m_scan = {}
    m_sp = {}
    m_u = {}
    m_ydone = {}
    m_bm = {}
    m_dtmm = {}
    m_ytr = {}
    m_oadd = {}
    m_st = {}
    for tau in range(NT_LIM):
        r0 = tau * TB
        for tt in range(NTT):
            ld.inc(SY.dma_start(arena[:, tt * D : (tt + 1) * D],
                                d_x[r0 + tt * 128 : r0 + (tt + 1) * 128, :]), 16)
        m_ld_x = ld.v
        V.wait_ge(ld.h, m_ld_x)
        if A_STAGE < 2:
            nc.all_engine_barrier()
            continue
        XT = [arena[:, tt * D : (tt + 1) * D] for tt in range(NTT)]
        cl = lambda j, tt: cols[:, 4 * j + tt : 4 * j + tt + 1]
        for tt in range(NTT):
            V.tensor_reduce(cl(0, tt), XT[tt], AX.X, Op.add)
        for tt in range(NTT):
            if NOACC:
                V.tensor_reduce(cl(1, tt), XT[tt], AX.X, Op.add)
            else:
                V.scalar_tensor_tensor(h_all[:, (tt & 1) * D : (tt & 1) * D + D],
                                       XT[tt], 1.0, XT[tt],
                                       Op.mult, Op.mult, accum_out=cl(1, tt))
        for tt in range(NTT):
            V.tensor_scalar(cl(2, tt), cl(0, tt), 1.0 / D, None, Op.mult)
        for tt in range(NTT):
            V.tensor_scalar(cl(3, tt), cl(1, tt), 1.0 / D, None, Op.mult)
        for tt in range(NTT):
            V.scalar_tensor_tensor(cl(4, tt), cl(2, tt), 1.0, cl(2, tt),
                                   Op.mult, Op.mult)
        for tt in range(NTT):
            V.tensor_tensor(cl(5, tt), cl(3, tt), cl(4, tt), Op.subtract)
        m1 = dve.inc(V.engine_nop())
        if A_STAGE < 3:
            nc.all_engine_barrier()
            continue
        A.wait_ge(dve.h, m1)
        for tt in range(NTT):
            A.activation(cl(6, tt), cl(5, tt), AF.Ln, bias=eps_sb[:, 0:1],
                         scale=1.0)
        m2 = act.v
        for tt in range(NTT):
            m2 = act.inc(A.activation(cl(7, tt), cl(6, tt), AF.Exp, bias=0.0,
                                      scale=-0.5))
        V.wait_ge(act.h, m2)
        for tt in range(NTT):
            V.scalar_tensor_tensor(cl(8, tt), cl(2, tt), -1.0, cl(7, tt),
                                   Op.mult, Op.mult)
        for tt in range(NTT):
            V.tensor_scalar(XT[tt], XT[tt], cl(7, tt), cl(8, tt), Op.mult, Op.add)
        m_xn = dve.inc(V.engine_nop())
        if A_STAGE < 4:
            nc.all_engine_barrier()
            continue

        T.wait_ge(dve.h, m_xn)
        m_ev16 = {}
        m_evf = {}
        for dc in range(NDF):
            slot = dc & 1
            ptp = (p0, p1)[slot]
            if dc >= 2:
                T.wait_ge(dve.h, m_evf[dc - 2])
            for tt in range(NTT):
                ti = T.transpose(ptp[:, tt * 128 : (tt + 1) * 128],
                                 arena[:, tt * D + dc * 128 : tt * D + (dc + 1) * 128],
                                 idf[:])
            m4 = pe.inc(ti)
            V.wait_ge(pe.h, m4)
            V.tensor_copy(xnT16[:, slot * TB : (slot + 1) * TB], ptp[:])
            if dc < NDC:
                V.tensor_copy(xnT[:, dc * TB : (dc + 1) * TB], ptp[:])
            m_evf[dc] = m_ev16[dc] = dve.inc(V.engine_nop())
            T.wait_ge(dve.h, m_ev16[dc])
            w = wcat16.rearrange("p (j c) -> p j c", j=NDF)[:, dc, :]
            rhs = xnT16[:, slot * TB : (slot + 1) * TB]
            T.matmul(p2[:], w[:, 0:128], rhs, start=(dc == 0), stop=(dc == NDF - 1))
            T.matmul(p3[0:32, :], w[:, 128:160], rhs, start=(dc == 0),
                     stop=(dc == NDF - 1))
            mi = T.matmul(p4[0:NE, :], w[:, 160:224], rhs, start=(dc == 0),
                          stop=(dc == NDF - 1))
        m5 = pe.inc(mi)
        A.wait_ge(pe.h, m5)
        A.activation(xp16[:], p2[:], AF.Copy, bias=0.0, scale=1.0)
        A.activation(bc32[:], p3[0:32, :], AF.Copy, bias=0.0, scale=1.0)
        m7 = act.inc(A.activation(lgrm[:], p4[0:NE, :], AF.Identity,
                                  bias=rb_sb[:, 0:1], scale=1.0))
        if A_STAGE < 5:
            nc.all_engine_barrier()
            continue
        T.wait_ge(act.h, m7)
        for tt in range(NTT):
            ti = T.transpose(p5[:, tt * NE : (tt + 1) * NE],
                             lgrm[:, tt * 128 : (tt + 1) * 128], idf[0:NE, 0:NE])
        m8 = pe.inc(ti)
        V.wait_ge(pe.h, m8)
        V.tensor_copy(lgtm[:], p5[:, 0 : NTT * NE])
        E_ = lambda tt: etm[:, tt * NE : (tt + 1) * NE]
        L_ = lambda tt: lgtm[:, tt * NE : (tt + 1) * NE]
        M_ = lambda tt: msk[:, tt * NE : (tt + 1) * NE]
        T8 = lambda tt: t8[:, tt * 8 : (tt + 1) * 8]
        W_ = lambda tt: wdtm[:, tt * 72 : tt * 72 + NE + 1]
        for tt in range(NTT):
            V.tensor_reduce(cl(0, tt), L_(tt), AX.X, Op.max)
        for tt in range(NTT):
            V.tensor_scalar(cl(1, tt), cl(0, tt), -1.0, None, Op.mult)
        m9 = dve.inc(V.engine_nop())
        A.wait_ge(dve.h, m9)
        m10 = act.v
        for tt in range(NTT):
            m10 = act.inc(A.activation(E_(tt), L_(tt), AF.Exp,
                                       bias=cl(1, tt), scale=1.0))
        V.wait_ge(act.h, m10)
        for tt in range(NTT):
            V.tensor_reduce(cl(2, tt), E_(tt), AX.X, Op.add)
        for tt in range(NTT):
            V.max(T8(tt), E_(tt))
        for tt in range(NTT):
            V.reciprocal(cl(3, tt), cl(2, tt))
        for tt in range(NTT):
            V.tensor_scalar(M_(tt), E_(tt), t8[:, tt * 8 + 7 : tt * 8 + 8],
                            None, Op.is_ge)
        for tt in range(NTT):
            V.tensor_reduce(cl(4, tt), T8(tt), AX.X, Op.add)
        for tt in range(NTT):
            V.tensor_tensor(wdtm[:, tt * 72 + NE : tt * 72 + NE + 1],
                            cl(4, tt), cl(3, tt), Op.mult)
        for tt in range(NTT):
            V.scalar_tensor_tensor(wdtm[:, tt * 72 : tt * 72 + NE],
                                   E_(tt), cl(3, tt), M_(tt), Op.mult, Op.mult)
        m11 = dve.inc(V.engine_nop())
        T.wait_ge(dve.h, m11)
        for tt in range(NTT):
            ti = T.transpose(p5[0 : NE + 1, tt * 128 : tt * 128 + 128],
                             W_(tt), idf[:])
        m12 = pe.inc(ti)
        A.wait_ge(pe.h, m12)
        for tt in range(NTT):
            A.activation(wd16[:, tt * 128 : (tt + 1) * 128],
                         p5[0:NE, tt * 128 : tt * 128 + 128], AF.Copy,
                         bias=0.0, scale=1.0)
        m12b = act.v
        for tt in range(NTT):
            m12b = act.inc(A.activation(wsrow[0:1, tt * 128 : (tt + 1) * 128],
                                        p5[NE : NE + 1, tt * 128 : tt * 128 + 128],
                                        AF.Copy, bias=0.0, scale=1.0))
        if A_STAGE < 6:
            nc.all_engine_barrier()
            continue
        T.wait_ge(act.h, m12b)
        T.wait_ge(dve.h, m_evf[NDC - 1])
        mi = T.matmul(p6[:], ones[:], wsrow[:], start=True, stop=True)
        m13 = pe.inc(mi)
        A.wait_ge(pe.h, m13)
        m_ws = act.inc(A.activation(wsb[:], p6[:], AF.Copy, bias=0.0, scale=1.0))
        m_bc_arena = {}
        m_bc_mm = {}
        for n in range(32):
            pbb = (p6, p7)[n & 1]
            if n >= 2:
                V.wait_ge(pe.h, m_bc_mm[n - 2])
            V.tensor_scalar(sel[:, (n & 1) * 128 : (n & 1) * 128 + 128],
                            iopf[:], n, None, Op.is_equal)
            msel = dve.inc(V.engine_nop())
            T.wait_ge(dve.h, msel)
            if n == 0:
                T.wait_ge(act.h, m_ws)
            if n >= 2:
                T.wait_ge(act.h, m_bc_arena[n - 2])
            mi = T.matmul(pbb[:], sel[:, (n & 1) * 128 : (n & 1) * 128 + 128],
                          bc32[:], start=True, stop=True)
            mm = m_bc_mm[n] = pe.inc(mi)
            A.wait_ge(pe.h, mm)
            m_bc_arena[n] = act.inc(A.activation(
                arena[:, n * TB : (n + 1) * TB], pbb[:], AF.Copy,
                bias=0.0, scale=1.0))

        nc.all_engine_barrier()

        for dc in range(DC_LIM):
            g = tau * NDC + dc
            sl2 = dc & 1
            pdt = (p0, p1)[sl2]
            pbm = (p2, p3)[sl2]
            ptpy = (p4, p5)[sl2]
            tcol = slice(dc * TB, (dc + 1) * TB)
            dsl = slice(sl2 * TB, (sl2 + 1) * TB)

            if g >= 2:
                T.wait_ge(act.h, m_sp[g - 2])
            mi = T.matmul(pdt[:], dtw16[:, dc * 128 : (dc + 1) * 128], xp16[:],
                          start=True, stop=True)
            m_dtmm[g] = pe.inc(mi)
            if g >= 2:
                T.wait_ge(dve.h, m_ydone[g - 2])
            mi = T.matmul(pbm[:], eb16[:, dc * 128 : (dc + 1) * 128], wd16[:],
                          start=True, stop=True)
            m_bm[g] = pe.inc(mi)

            A.wait_ge(pe.h, m_dtmm[g])
            if g >= 2:
                A.wait_ge(dve.h, m_u[g - 2])
            A.activation(sp_st[:, dsl], pdt[:], AF.Exp,
                         bias=bvec[:, dc : dc + 1], scale=1.0)
            m_sp[g] = act.inc(A.activation(dt_sb[:, dsl], sp_st[:, dsl],
                                            AF.Ln, bias=1.0, scale=1.0))

            V.wait_ge(act.h, m_sp[g])
            if g >= 2:
                V.wait_ge(pe.h, m_ytr.get(g - 2, 0))
                if USE_GPSIMD:
                    V.wait_ge(gp.h, m_gp[(g - 2, 13)])
            V.tensor_tensor(u_sb[:, dsl], dt_sb[:, dsl], xnT[:, tcol], Op.mult)
            V.scalar_tensor_tensor(y_sb[:, dsl], wsb[:], dvec[:, dc : dc + 1],
                                   xnT[:, tcol], Op.add, Op.mult)
            V.tensor_tensor(db_sb[:, 0:TB], u_sb[:, dsl], arena[:, 0:TB], Op.mult)
            m_u[g] = dve.inc(V.engine_nop())

            NGP = 14 if USE_GPSIMD else 0
            if USE_GPSIMD:
                G.wait_ge(dve.h, m_u[g])
                for k in (0, 1):
                    if (g - 1, k + 12) in m_scan:
                        G.wait_ge(dve.h, m_scan[(g - 1, k + 12)])
                    gi = G.tensor_tensor(db_sb[:, (k & 3) * TB : (k & 3) * TB + TB],
                                         u_sb[:, dsl],
                                         arena[:, k * TB : (k + 1) * TB], Op.mult)
                    m_gp[(g, k)] = gp.inc(gi)

            for n in range(N):
                par = n & 1
                pda = (p6, p7)[par]
                if n >= 2:
                    A.wait_ge(dve.h, m_scan[(g, n - 2)])
                else:
                    A.wait_ge(dve.h, m_u[g])
                m_da = act.inc(A.activation(
                    pda[:], dt_sb[:, dsl], AF.Exp, bias=0.0,
                    scale=a_sb[:, dc * N + n : dc * N + n + 1]))
                k = n + 2
                if k < N:
                    if k < NGP:
                        G.wait_ge(dve.h, m_scan[(g, k - 4)] if k >= 4 else
                                  m_u[g])
                        gi = G.tensor_tensor(
                            db_sb[:, (k & 3) * TB : (k & 3) * TB + TB],
                            u_sb[:, dsl], arena[:, k * TB : (k + 1) * TB],
                            Op.mult)
                        m_gp[(g, k)] = gp.inc(gi)
                    else:
                        V.tensor_tensor(
                            db_sb[:, (k & 3) * TB : (k & 3) * TB + TB],
                            u_sb[:, dsl], arena[:, k * TB : (k + 1) * TB],
                            Op.mult)
                if not USE_GPSIMD and n == 1:
                    V.tensor_tensor(db_sb[:, (n & 3) * TB : (n & 3) * TB + TB],
                                    u_sb[:, dsl],
                                    arena[:, n * TB : (n + 1) * TB], Op.mult)
                V.wait_ge(act.h, m_da)
                if n < NGP:
                    V.wait_ge(gp.h, m_gp[(g, n)])
                V.tensor_tensor_scan(h_all[:, n * TB : (n + 1) * TB],
                                     pda[:],
                                     db_sb[:, (n & 3) * TB : (n & 3) * TB + TB],
                                     hst[:, dc * N + n : dc * N + n + 1],
                                     Op.mult, Op.add)
                m_scan[(g, n)] = dve.inc(V.engine_nop())

            V.engine_nop()
            V.tensor_copy(hst[:, dc * N : (dc + 1) * N],
                          h_all.rearrange("p (n t) -> p n t", n=N)[:, :, TB - 1])
            V.tensor_tensor(h_all[:], h_all[:], arena[:, 16 * TB : 32 * TB],
                            Op.mult)
            w = 8
            while w >= 1:
                V.tensor_tensor(h_all[:, 0 : w * TB], h_all[:, 0 : w * TB],
                                h_all[:, w * TB : 2 * w * TB], Op.add)
                w //= 2
            V.tensor_tensor(y_sb[:, dsl], y_sb[:, dsl], h_all[:, 0:TB], Op.add)
            V.wait_ge(pe.h, m_bm[g])
            V.tensor_tensor(y_sb[:, dsl], y_sb[:, dsl], pbm[:], Op.add)
            m_ydone[g] = dve.inc(V.engine_nop())

            if (g - 2) in m_oadd:
                SY.wait_ge(dve.h, m_oadd[g - 2])
            for tt in range(NTT):
                ld.inc(SY.dma_start(
                    xres[:, sl2 * TB + tt * 128 : sl2 * TB + (tt + 1) * 128],
                    bass.AP(d_xh, (r0 + tt * 128) * DH + dc * 128,
                            [[DH, 128], [1, 128]])), 16)
            m_ld_res = ld.v

            T.wait_ge(dve.h, m_ydone[g])
            if (g - 2) in m_oadd:
                T.wait_ge(dve.h, m_oadd[g - 2])
            for k in range(4):
                ti = T.transpose(
                    ptpy[:, k * 128 : (k + 1) * 128],
                    y_sb[:, sl2 * TB + k * 128 : sl2 * TB + (k + 1) * 128],
                    idf[:])
            m_ytr[g] = pe.inc(ti)
            V.wait_ge(pe.h, m_ytr[g])
            V.wait_ge(ld.h, m_ld_res)
            if (g - 2) in m_st:
                V.wait_ge(st.h, m_st[g - 2])
            V.tensor_tensor(outsb[:, dsl], xres[:, dsl], ptpy[:], Op.add)
            m_oadd[g] = dve.inc(V.engine_nop())
            SY.wait_ge(dve.h, m_oadd[g])
            for tt in range(NTT):
                st.inc(SY.dma_start(
                    bass.AP(d_out, (r0 + tt * 128) * DH + dc * 128,
                            [[DH, 128], [1, 128]]),
                    outsb[:, sl2 * TB + tt * 128 : sl2 * TB + (tt + 1) * 128]), 16)
            m_st[g] = st.v

        nc.all_engine_barrier()

    SY.wait_ge(st.h, st.v)
    SY.wait_ge(ld.h, ld.v)
    nc.all_engine_barrier()
    return nc


def kernel(L0_attn_x, L0_attn_A_log, L0_attn_D, L0_attn_x_proj_w,
           L0_attn_dt_proj_w, L0_attn_dt_proj_b, L0_ffn_router_weight,
           L0_ffn_router_bias, L0_ffn_expert_bias):
    x = np.asarray(L0_attn_x, dtype=np.float32)
    wcat_rows = np.concatenate([np.asarray(L0_attn_x_proj_w, np.float32),
                                np.asarray(L0_ffn_router_weight, np.float32)],
                               axis=0).T
    in_maps = []
    for c in range(8):
        b, dh = c >> 1, c & 1
        sl = slice(dh * DH, (dh + 1) * DH)
        perm = np.r_[dh * DH : (dh + 1) * DH, (1 - dh) * DH : (2 - dh) * DH]
        wc = wcat_rows[perm]
        wc = np.ascontiguousarray(
            wc.reshape(NDF, 128, 224).transpose(1, 0, 2).reshape(128, NDF * 224))
        in_maps.append(dict(
            x=np.ascontiguousarray(x[b][:, perm]),
            xh=np.ascontiguousarray(x[b][:, sl]),
            wcat=wc,
            dtw=np.ascontiguousarray(np.asarray(L0_attn_dt_proj_w, np.float32)[sl].T),
            alog=np.ascontiguousarray(
                np.asarray(L0_attn_A_log, np.float32)[sl]
                .reshape(NDC, 128, N).transpose(1, 0, 2).reshape(128, NDC * N)),
            dvec=np.ascontiguousarray(
                np.asarray(L0_attn_D, np.float32)[sl].reshape(NDC, 128).T),
            bvec=np.ascontiguousarray(
                np.asarray(L0_attn_dt_proj_b, np.float32)[sl].reshape(NDC, 128).T),
            rb=np.ascontiguousarray(
                np.asarray(L0_ffn_router_bias, np.float32)[:, None]),
            eb=np.ascontiguousarray(
                np.asarray(L0_ffn_expert_bias, np.float32)[:, sl]),
        ))
    nc = build_program()
    res = run_bass_kernel_spmd(nc, in_maps, core_ids=list(range(8)))
    out = np.empty((B, S, D), dtype=np.float32)
    for c in range(8):
        b, dh = c >> 1, c & 1
        out[b, :, dh * DH : (dh + 1) * DH] = res.results[c]["out"]
    return out


# version 07
# speedup vs baseline: 1.2690x; 1.1186x over previous
import numpy as np
import concourse.bass as bass
import concourse.mybir as mybir
from concourse.bass_utils import run_bass_kernel_spmd
from concourse.alu_op_type import AluOpType as Op

F32 = mybir.dt.float32
BF16 = mybir.dt.bfloat16
I32 = mybir.dt.int32
AF = mybir.ActivationFunctionType
AX = mybir.AxisListType

B, S, D, N, RNK, NE = 4, 2048, 4096, 16, 128, 64
DH = D // 2
TB = 512
NT = S // TB
NDC = DH // 128
NDF = D // 128
NTT = TB // 128
USE_GPSIMD = True
NT_LIM = NT
DC_LIM = NDC
A_STAGE = 6
NOACC = False


class Ctr:
    def __init__(self, nc, name):
        self.h = nc.alloc_semaphore(name)
        self.v = 0

    def inc(self, instr, k=1):
        instr.then_inc(self.h, k)
        self.v += k
        return self.v


def build_program():
    nc = bass.Bass("TRN2", target_bir_lowering=False)

    d_x = nc.dram_tensor("x", [S, D], F32, kind="ExternalInput")
    d_xh = nc.dram_tensor("xh", [S, DH], F32, kind="ExternalInput")
    d_wcat = nc.dram_tensor("wcat", [128, NDF * 224], F32, kind="ExternalInput")
    d_dtw = nc.dram_tensor("dtw", [RNK, DH], F32, kind="ExternalInput")
    d_alog = nc.dram_tensor("alog", [128, NDC * N], F32, kind="ExternalInput")
    d_dvec = nc.dram_tensor("dvec", [128, NDC], F32, kind="ExternalInput")
    d_bvec = nc.dram_tensor("bvec", [128, NDC], F32, kind="ExternalInput")
    d_rb = nc.dram_tensor("rb", [NE, 1], F32, kind="ExternalInput")
    d_eb = nc.dram_tensor("eb", [NE, DH], F32, kind="ExternalInput")
    d_out = nc.dram_tensor("out", [S, DH], F32, kind="ExternalOutput")

    arena = nc.alloc_sbuf_tensor("arena", [128, NTT * D], F32)
    xnT = nc.alloc_sbuf_tensor("xnT", [128, NDC * TB], F32)
    xnT16 = nc.alloc_sbuf_tensor("xnT16", [128, 2 * TB], BF16)
    h_all = nc.alloc_sbuf_tensor("h_all", [128, N * TB], F32)
    wcat16 = nc.alloc_sbuf_tensor("wcat16", [128, NDF * 224], BF16)
    dtw16 = nc.alloc_sbuf_tensor("dtw16", [128, DH], BF16)
    eb16 = nc.alloc_sbuf_tensor("eb16", [NE, DH], BF16)
    a_sb = nc.alloc_sbuf_tensor("a_sb", [128, NDC * N], F32)
    dvec = nc.alloc_sbuf_tensor("dvec_sb", [128, NDC], F32)
    bvec = nc.alloc_sbuf_tensor("bvec_sb", [128, NDC], F32)
    rb_sb = nc.alloc_sbuf_tensor("rb_sb", [NE, 1], F32)
    wd16 = nc.alloc_sbuf_tensor("wd16", [NE, TB], BF16)
    xp16 = nc.alloc_sbuf_tensor("xp16", [128, TB], BF16)
    bc32 = nc.alloc_sbuf_tensor("bc32", [32, TB], F32)
    lgrm = nc.alloc_sbuf_tensor("lgrm", [NE, TB], F32)
    lgtm = nc.alloc_sbuf_tensor("lgtm", [128, NTT * NE], F32)
    etm = nc.alloc_sbuf_tensor("etm", [128, NTT * NE], F32)
    msk = nc.alloc_sbuf_tensor("msk", [128, NTT * NE], F32)
    wdtm = nc.alloc_sbuf_tensor("wdtm", [128, NTT * 72], F32)
    t8 = nc.alloc_sbuf_tensor("t8", [128, NTT * 8], F32)
    cols = nc.alloc_sbuf_tensor("cols", [128, 48], F32)
    spare = nc.alloc_sbuf_tensor("spare", [128, 1], F32)
    wsrow = nc.alloc_sbuf_tensor("wsrow", [1, TB], F32)
    wsb = nc.alloc_sbuf_tensor("wsb", [128, TB], F32)
    dt_sb = nc.alloc_sbuf_tensor("dt_sb", [128, 2 * TB], F32)
    sp_st = nc.alloc_sbuf_tensor("sp_st", [128, 2 * TB], F32)
    u_sb = nc.alloc_sbuf_tensor("u_sb", [128, 2 * TB], F32)
    y_sb = nc.alloc_sbuf_tensor("y_sb", [128, 2 * TB], F32)
    da_sb = nc.alloc_sbuf_tensor("da_sb", [128, 2 * TB], F32)
    db_sb = nc.alloc_sbuf_tensor("db_sb", [128, 4 * TB], F32)
    hst = nc.alloc_sbuf_tensor("hst", [128, NDC * N], F32)
    xres = nc.alloc_sbuf_tensor("xres", [128, 2 * TB], F32)
    outsb = nc.alloc_sbuf_tensor("outsb", [128, 2 * TB], F32)
    idf = nc.alloc_sbuf_tensor("idf", [128, 128], F32)
    idi = nc.alloc_sbuf_tensor("idi", [128, 128], I32)
    iopf = nc.alloc_sbuf_tensor("iopf", [32, 128], I32)
    sel = nc.alloc_sbuf_tensor("sel", [32, 2 * 128], F32)
    ones = nc.alloc_sbuf_tensor("ones", [1, 128], F32)
    eps_sb = nc.alloc_sbuf_tensor("eps_sb", [128, 1], F32)

    p0 = nc.alloc_psum_tensor("p0", [128, 512], F32)
    p1 = nc.alloc_psum_tensor("p1", [128, 512], F32)
    p2 = nc.alloc_psum_tensor("p2", [128, 512], F32)
    p3 = nc.alloc_psum_tensor("p3", [128, 512], F32)
    p4 = nc.alloc_psum_tensor("p4", [128, 512], F32)
    p5 = nc.alloc_psum_tensor("p5", [128, 512], F32)
    p6 = nc.alloc_psum_tensor("p6", [128, 512], F32)
    p7 = nc.alloc_psum_tensor("p7", [128, 512], F32)

    ld = Ctr(nc, "c_ld")
    pe = Ctr(nc, "c_pe")
    act = Ctr(nc, "c_act")
    dve = Ctr(nc, "c_dve")
    st = Ctr(nc, "c_st")
    gp = Ctr(nc, "c_gp")

    nc.all_engine_barrier()
    nc.gpsimd.sem_clear(range(ld.h.num, gp.h.num + 1))
    nc.all_engine_barrier()

    V, A, T, G, SY = nc.vector, nc.scalar, nc.tensor, nc.gpsimd, nc.sync

    ld.inc(SY.dma_start(arena[:, 0 : NDF * 224], d_wcat[:, :]), 16)
    ld.inc(SY.dma_start(h_all[:, 0:DH], d_dtw[:, :]), 16)
    ld.inc(SY.dma_start(h_all[0:NE, DH : 2 * DH], d_eb[:, :]), 16)
    ld.inc(SY.dma_start(cols[0:NE, 0:1], d_rb[:, :]), 16)
    ld.inc(SY.dma_start(a_sb[:, :], d_alog[:, :]), 16)
    ld.inc(SY.dma_start(dvec[:, :], d_dvec[:, :]), 16)
    ld.inc(SY.dma_start(bvec[:, :], d_bvec[:, :]), 16)

    G.iota(idi[:], pattern=[[1, 128]], base=0, channel_multiplier=-1)
    gp.inc(G.iota(iopf[:], pattern=[[0, 128]], base=0, channel_multiplier=1))

    V.wait_ge(ld.h, ld.v)
    V.wait_ge(gp.h, gp.v)
    V.tensor_scalar(idf[:], idi[:], 0, None, Op.is_equal)
    V.memset(ones[:], 1.0)
    V.memset(eps_sb[:], 1e-5)
    V.memset(spare[:], 0.0)
    V.memset(hst[:], 0.0)
    V.tensor_copy(wcat16[:], arena[:, 0 : NDF * 224])
    V.tensor_copy(dtw16[:], h_all[:, 0:DH])
    V.tensor_copy(eb16[:], h_all[0:NE, DH : 2 * DH])
    V.tensor_copy(rb_sb[:], cols[0:NE, 0:1])
    mv = dve.inc(V.engine_nop())
    A.wait_ge(dve.h, mv)
    ma = act.inc(A.activation(a_sb[:], a_sb[:], AF.Exp, bias=0.0, scale=1.0))
    V.wait_ge(act.h, ma)
    V.tensor_scalar(a_sb[:], a_sb[:], -1.0, None, Op.mult)
    dve.inc(V.engine_nop())

    nc.all_engine_barrier()

    m_gp = {}
    m_scan = {}
    m_sp = {}
    m_u = {}
    m_ydone = {}
    m_bm = {}
    m_dtmm = {}
    m_ytr = {}
    m_oadd = {}
    m_st = {}
    for tau in range(NT_LIM):
        r0 = tau * TB
        for tt in range(NTT):
            ld.inc(SY.dma_start(arena[:, tt * D : (tt + 1) * D],
                                d_x[r0 + tt * 128 : r0 + (tt + 1) * 128, :]), 16)
        m_ld_x = ld.v
        V.wait_ge(ld.h, m_ld_x)
        if A_STAGE < 2:
            nc.all_engine_barrier()
            continue
        XT = [arena[:, tt * D : (tt + 1) * D] for tt in range(NTT)]
        cl = lambda j, tt: cols[:, 4 * j + tt : 4 * j + tt + 1]
        for tt in range(NTT):
            V.tensor_reduce(cl(0, tt), XT[tt], AX.X, Op.add)
        for tt in range(NTT):
            if NOACC:
                V.tensor_reduce(cl(1, tt), XT[tt], AX.X, Op.add)
            else:
                V.scalar_tensor_tensor(h_all[:, (tt & 1) * D : (tt & 1) * D + D],
                                       XT[tt], 1.0, XT[tt],
                                       Op.mult, Op.mult, accum_out=cl(1, tt))
        for tt in range(NTT):
            V.tensor_scalar(cl(2, tt), cl(0, tt), 1.0 / D, None, Op.mult)
        for tt in range(NTT):
            V.tensor_scalar(cl(3, tt), cl(1, tt), 1.0 / D, None, Op.mult)
        for tt in range(NTT):
            V.scalar_tensor_tensor(cl(4, tt), cl(2, tt), 1.0, cl(2, tt),
                                   Op.mult, Op.mult)
        for tt in range(NTT):
            V.tensor_tensor(cl(5, tt), cl(3, tt), cl(4, tt), Op.subtract)
        m1 = dve.inc(V.engine_nop())
        if A_STAGE < 3:
            nc.all_engine_barrier()
            continue
        A.wait_ge(dve.h, m1)
        for tt in range(NTT):
            A.activation(cl(6, tt), cl(5, tt), AF.Ln, bias=eps_sb[:, 0:1],
                         scale=1.0)
        m2 = act.v
        for tt in range(NTT):
            m2 = act.inc(A.activation(cl(7, tt), cl(6, tt), AF.Exp, bias=0.0,
                                      scale=-0.5))
        V.wait_ge(act.h, m2)
        for tt in range(NTT):
            V.scalar_tensor_tensor(cl(8, tt), cl(2, tt), -1.0, cl(7, tt),
                                   Op.mult, Op.mult)
        for tt in range(NTT):
            V.tensor_scalar(XT[tt], XT[tt], cl(7, tt), cl(8, tt), Op.mult, Op.add)
        m_xn = dve.inc(V.engine_nop())
        if A_STAGE < 4:
            nc.all_engine_barrier()
            continue

        T.wait_ge(dve.h, m_xn)
        m_tr = {}
        m_evf = {}
        def emit_trans(dc):
            ptp = (p0, p1)[dc & 1]
            if dc >= 2:
                T.wait_ge(dve.h, m_evf[dc - 2])
            for tt in range(NTT):
                ti = T.transpose(ptp[:, tt * 128 : (tt + 1) * 128],
                                 arena[:, tt * D + dc * 128 : tt * D + (dc + 1) * 128],
                                 idf[:])
            m_tr[dc] = pe.inc(ti)
        emit_trans(0)
        for dc in range(NDF):
            slot = dc & 1
            ptp = (p0, p1)[slot]
            V.wait_ge(pe.h, m_tr[dc])
            V.tensor_copy(xnT16[:, slot * TB : (slot + 1) * TB], ptp[:])
            if dc < NDC:
                V.tensor_copy(xnT[:, dc * TB : (dc + 1) * TB], ptp[:])
            m_evf[dc] = dve.inc(V.engine_nop())
            if dc + 1 < NDF:
                emit_trans(dc + 1)
            T.wait_ge(dve.h, m_evf[dc])
            w = wcat16.rearrange("p (j c) -> p j c", j=NDF)[:, dc, :]
            rhs = xnT16[:, slot * TB : (slot + 1) * TB]
            T.matmul(p2[:], w[:, 0:128], rhs, start=(dc == 0), stop=(dc == NDF - 1))
            T.matmul(p3[0:32, :], w[:, 128:160], rhs, start=(dc == 0),
                     stop=(dc == NDF - 1))
            mi = T.matmul(p4[0:NE, :], w[:, 160:224], rhs, start=(dc == 0),
                          stop=(dc == NDF - 1))
        m5 = pe.inc(mi)
        A.wait_ge(pe.h, m5)
        A.activation(xp16[:], p2[:], AF.Copy, bias=0.0, scale=1.0)
        A.activation(bc32[:], p3[0:32, :], AF.Copy, bias=0.0, scale=1.0)
        m7 = act.inc(A.activation(lgrm[:], p4[0:NE, :], AF.Identity,
                                  bias=rb_sb[:, 0:1], scale=1.0))
        if A_STAGE < 5:
            nc.all_engine_barrier()
            continue
        T.wait_ge(act.h, m7)
        for tt in range(NTT):
            ti = T.transpose(p5[:, tt * NE : (tt + 1) * NE],
                             lgrm[:, tt * 128 : (tt + 1) * 128], idf[0:NE, 0:NE])
        m8 = pe.inc(ti)
        V.wait_ge(pe.h, m8)
        V.tensor_copy(lgtm[:], p5[:, 0 : NTT * NE])
        E_ = lambda tt: etm[:, tt * NE : (tt + 1) * NE]
        L_ = lambda tt: lgtm[:, tt * NE : (tt + 1) * NE]
        M_ = lambda tt: msk[:, tt * NE : (tt + 1) * NE]
        T8 = lambda tt: t8[:, tt * 8 : (tt + 1) * 8]
        W_ = lambda tt: wdtm[:, tt * 72 : tt * 72 + NE + 1]
        for tt in range(NTT):
            V.tensor_reduce(cl(0, tt), L_(tt), AX.X, Op.max)
        for tt in range(NTT):
            V.tensor_scalar(cl(1, tt), cl(0, tt), -1.0, None, Op.mult)
        m9 = dve.inc(V.engine_nop())
        A.wait_ge(dve.h, m9)
        m10 = act.v
        for tt in range(NTT):
            m10 = act.inc(A.activation(E_(tt), L_(tt), AF.Exp,
                                       bias=cl(1, tt), scale=1.0))
        V.wait_ge(act.h, m10)
        for tt in range(NTT):
            V.tensor_reduce(cl(2, tt), E_(tt), AX.X, Op.add)
        for tt in range(NTT):
            V.max(T8(tt), E_(tt))
        for tt in range(NTT):
            V.reciprocal(cl(3, tt), cl(2, tt))
        for tt in range(NTT):
            V.tensor_scalar(M_(tt), E_(tt), t8[:, tt * 8 + 7 : tt * 8 + 8],
                            None, Op.is_ge)
        for tt in range(NTT):
            V.tensor_reduce(cl(4, tt), T8(tt), AX.X, Op.add)
        for tt in range(NTT):
            V.tensor_tensor(wdtm[:, tt * 72 + NE : tt * 72 + NE + 1],
                            cl(4, tt), cl(3, tt), Op.mult)
        for tt in range(NTT):
            V.scalar_tensor_tensor(wdtm[:, tt * 72 : tt * 72 + NE],
                                   E_(tt), cl(3, tt), M_(tt), Op.mult, Op.mult)
        m11 = dve.inc(V.engine_nop())
        T.wait_ge(dve.h, m11)
        for tt in range(NTT):
            ti = T.transpose(p5[0 : NE + 1, tt * 128 : tt * 128 + 128],
                             W_(tt), idf[:])
        m12 = pe.inc(ti)
        A.wait_ge(pe.h, m12)
        for tt in range(NTT):
            A.activation(wd16[:, tt * 128 : (tt + 1) * 128],
                         p5[0:NE, tt * 128 : tt * 128 + 128], AF.Copy,
                         bias=0.0, scale=1.0)
        m12b = act.v
        for tt in range(NTT):
            m12b = act.inc(A.activation(wsrow[0:1, tt * 128 : (tt + 1) * 128],
                                        p5[NE : NE + 1, tt * 128 : tt * 128 + 128],
                                        AF.Copy, bias=0.0, scale=1.0))
        if A_STAGE < 6:
            nc.all_engine_barrier()
            continue
        T.wait_ge(act.h, m12b)
        T.wait_ge(dve.h, m_evf[NDC - 1])
        mi = T.matmul(p6[:], ones[:], wsrow[:], start=True, stop=True)
        m13 = pe.inc(mi)
        A.wait_ge(pe.h, m13)
        m_ws = act.inc(A.activation(wsb[:], p6[:], AF.Copy, bias=0.0, scale=1.0))
        m_bc_arena = {}
        m_bc_mm = {}
        m_sel = {}
        def emit_sel(n):
            if n >= 2:
                V.wait_ge(pe.h, m_bc_mm[n - 2])
            V.tensor_scalar(sel[:, (n & 1) * 128 : (n & 1) * 128 + 128],
                            iopf[:], n, None, Op.is_equal)
            m_sel[n] = dve.inc(V.engine_nop())
        emit_sel(0)
        emit_sel(1)
        for n in range(32):
            pbb = (p6, p7)[n & 1]
            T.wait_ge(dve.h, m_sel[n])
            if n == 0:
                T.wait_ge(act.h, m_ws)
            if n >= 2:
                T.wait_ge(act.h, m_bc_arena[n - 2])
            mi = T.matmul(pbb[:], sel[:, (n & 1) * 128 : (n & 1) * 128 + 128],
                          bc32[:], start=True, stop=True)
            m_bc_mm[n] = pe.inc(mi)
            if n + 2 < 32:
                emit_sel(n + 2)
            A.wait_ge(pe.h, m_bc_mm[n])
            m_bc_arena[n] = act.inc(A.activation(
                arena[:, n * TB : (n + 1) * TB], pbb[:], AF.Copy,
                bias=0.0, scale=1.0))

        nc.all_engine_barrier()

        for dc in range(DC_LIM):
            g = tau * NDC + dc
            sl2 = dc & 1
            pdt = (p0, p1)[sl2]
            pbm = (p2, p3)[sl2]
            ptpy = (p4, p5)[sl2]
            tcol = slice(dc * TB, (dc + 1) * TB)
            dsl = slice(sl2 * TB, (sl2 + 1) * TB)

            if g >= 2:
                T.wait_ge(act.h, m_sp[g - 2])
            mi = T.matmul(pdt[:], dtw16[:, dc * 128 : (dc + 1) * 128], xp16[:],
                          start=True, stop=True)
            m_dtmm[g] = pe.inc(mi)
            if g >= 2:
                T.wait_ge(dve.h, m_ydone[g - 2])
            mi = T.matmul(pbm[:], eb16[:, dc * 128 : (dc + 1) * 128], wd16[:],
                          start=True, stop=True)
            m_bm[g] = pe.inc(mi)

            A.wait_ge(pe.h, m_dtmm[g])
            if g >= 2:
                A.wait_ge(dve.h, m_u[g - 2])
            A.activation(sp_st[:, dsl], pdt[:], AF.Exp,
                         bias=bvec[:, dc : dc + 1], scale=1.0)
            m_sp[g] = act.inc(A.activation(dt_sb[:, dsl], sp_st[:, dsl],
                                            AF.Ln, bias=1.0, scale=1.0))

            V.wait_ge(act.h, m_sp[g])
            if g >= 2:
                V.wait_ge(pe.h, m_ytr.get(g - 2, 0))
                if USE_GPSIMD:
                    V.wait_ge(gp.h, m_gp[(g - 2, N - 1)])
            V.tensor_tensor(u_sb[:, dsl], dt_sb[:, dsl], xnT[:, tcol], Op.mult)
            V.scalar_tensor_tensor(y_sb[:, dsl], wsb[:], dvec[:, dc : dc + 1],
                                   xnT[:, tcol], Op.add, Op.mult)
            V.tensor_tensor(db_sb[:, 0:TB], u_sb[:, dsl], arena[:, 0:TB], Op.mult)
            m_u[g] = dve.inc(V.engine_nop())

            NGP = 16 if USE_GPSIMD else 0
            if USE_GPSIMD:
                G.wait_ge(dve.h, m_u[g])
                for k in (0, 1):
                    if (g - 1, k + 12) in m_scan:
                        G.wait_ge(dve.h, m_scan[(g - 1, k + 12)])
                    gi = G.tensor_tensor(db_sb[:, (k & 3) * TB : (k & 3) * TB + TB],
                                         u_sb[:, dsl],
                                         arena[:, k * TB : (k + 1) * TB], Op.mult)
                    m_gp[(g, k)] = gp.inc(gi)

            for n in range(N):
                par = n & 1
                pda = (p6, p7)[par]
                if n >= 2:
                    A.wait_ge(dve.h, m_scan[(g, n - 2)])
                else:
                    A.wait_ge(dve.h, m_u[g])
                m_da = act.inc(A.activation(
                    pda[:], dt_sb[:, dsl], AF.Exp, bias=0.0,
                    scale=a_sb[:, dc * N + n : dc * N + n + 1]))
                k = n + 2
                if k < N:
                    if k < NGP:
                        G.wait_ge(dve.h, m_scan[(g, k - 4)] if k >= 4 else
                                  m_u[g])
                        gi = G.tensor_tensor(
                            db_sb[:, (k & 3) * TB : (k & 3) * TB + TB],
                            u_sb[:, dsl], arena[:, k * TB : (k + 1) * TB],
                            Op.mult)
                        m_gp[(g, k)] = gp.inc(gi)
                    else:
                        V.tensor_tensor(
                            db_sb[:, (k & 3) * TB : (k & 3) * TB + TB],
                            u_sb[:, dsl], arena[:, k * TB : (k + 1) * TB],
                            Op.mult)
                if not USE_GPSIMD and n == 1:
                    V.tensor_tensor(db_sb[:, (n & 3) * TB : (n & 3) * TB + TB],
                                    u_sb[:, dsl],
                                    arena[:, n * TB : (n + 1) * TB], Op.mult)
                V.wait_ge(act.h, m_da)
                if n < NGP:
                    V.wait_ge(gp.h, m_gp[(g, n)])
                V.tensor_tensor_scan(h_all[:, n * TB : (n + 1) * TB],
                                     pda[:],
                                     db_sb[:, (n & 3) * TB : (n & 3) * TB + TB],
                                     hst[:, dc * N + n : dc * N + n + 1],
                                     Op.mult, Op.add)
                m_scan[(g, n)] = dve.inc(V.engine_nop())

            V.engine_nop()
            V.tensor_copy(hst[:, dc * N : (dc + 1) * N],
                          h_all.rearrange("p (n t) -> p n t", n=N)[:, :, TB - 1])
            V.tensor_tensor(h_all[:], h_all[:], arena[:, 16 * TB : 32 * TB],
                            Op.mult)
            w = 8
            while w >= 1:
                V.tensor_tensor(h_all[:, 0 : w * TB], h_all[:, 0 : w * TB],
                                h_all[:, w * TB : 2 * w * TB], Op.add)
                w //= 2
            V.tensor_tensor(y_sb[:, dsl], y_sb[:, dsl], h_all[:, 0:TB], Op.add)
            V.wait_ge(pe.h, m_bm[g])
            V.tensor_tensor(y_sb[:, dsl], y_sb[:, dsl], pbm[:], Op.add)
            m_ydone[g] = dve.inc(V.engine_nop())

            if (g - 2) in m_oadd:
                SY.wait_ge(dve.h, m_oadd[g - 2])
            for tt in range(NTT):
                ld.inc(SY.dma_start(
                    xres[:, sl2 * TB + tt * 128 : sl2 * TB + (tt + 1) * 128],
                    bass.AP(d_xh, (r0 + tt * 128) * DH + dc * 128,
                            [[DH, 128], [1, 128]])), 16)
            m_ld_res = ld.v

            T.wait_ge(dve.h, m_ydone[g])
            if (g - 2) in m_oadd:
                T.wait_ge(dve.h, m_oadd[g - 2])
            for k in range(4):
                ti = T.transpose(
                    ptpy[:, k * 128 : (k + 1) * 128],
                    y_sb[:, sl2 * TB + k * 128 : sl2 * TB + (k + 1) * 128],
                    idf[:])
            m_ytr[g] = pe.inc(ti)
            V.wait_ge(pe.h, m_ytr[g])
            V.wait_ge(ld.h, m_ld_res)
            if (g - 2) in m_st:
                V.wait_ge(st.h, m_st[g - 2])
            V.tensor_tensor(outsb[:, dsl], xres[:, dsl], ptpy[:], Op.add)
            m_oadd[g] = dve.inc(V.engine_nop())
            SY.wait_ge(dve.h, m_oadd[g])
            for tt in range(NTT):
                st.inc(SY.dma_start(
                    bass.AP(d_out, (r0 + tt * 128) * DH + dc * 128,
                            [[DH, 128], [1, 128]]),
                    outsb[:, sl2 * TB + tt * 128 : sl2 * TB + (tt + 1) * 128]), 16)
            m_st[g] = st.v

        nc.all_engine_barrier()

    SY.wait_ge(st.h, st.v)
    SY.wait_ge(ld.h, ld.v)
    nc.all_engine_barrier()
    return nc


def kernel(L0_attn_x, L0_attn_A_log, L0_attn_D, L0_attn_x_proj_w,
           L0_attn_dt_proj_w, L0_attn_dt_proj_b, L0_ffn_router_weight,
           L0_ffn_router_bias, L0_ffn_expert_bias):
    x = np.asarray(L0_attn_x, dtype=np.float32)
    wcat_rows = np.concatenate([np.asarray(L0_attn_x_proj_w, np.float32),
                                np.asarray(L0_ffn_router_weight, np.float32)],
                               axis=0).T
    in_maps = []
    for c in range(8):
        b, dh = c >> 1, c & 1
        sl = slice(dh * DH, (dh + 1) * DH)
        perm = np.r_[dh * DH : (dh + 1) * DH, (1 - dh) * DH : (2 - dh) * DH]
        wc = wcat_rows[perm]
        wc = np.ascontiguousarray(
            wc.reshape(NDF, 128, 224).transpose(1, 0, 2).reshape(128, NDF * 224))
        in_maps.append(dict(
            x=np.ascontiguousarray(x[b][:, perm]),
            xh=np.ascontiguousarray(x[b][:, sl]),
            wcat=wc,
            dtw=np.ascontiguousarray(np.asarray(L0_attn_dt_proj_w, np.float32)[sl].T),
            alog=np.ascontiguousarray(
                np.asarray(L0_attn_A_log, np.float32)[sl]
                .reshape(NDC, 128, N).transpose(1, 0, 2).reshape(128, NDC * N)),
            dvec=np.ascontiguousarray(
                np.asarray(L0_attn_D, np.float32)[sl].reshape(NDC, 128).T),
            bvec=np.ascontiguousarray(
                np.asarray(L0_attn_dt_proj_b, np.float32)[sl].reshape(NDC, 128).T),
            rb=np.ascontiguousarray(
                np.asarray(L0_ffn_router_bias, np.float32)[:, None]),
            eb=np.ascontiguousarray(
                np.asarray(L0_ffn_expert_bias, np.float32)[:, sl]),
        ))
    nc = build_program()
    res = run_bass_kernel_spmd(nc, in_maps, core_ids=list(range(8)))
    out = np.empty((B, S, D), dtype=np.float32)
    for c in range(8):
        b, dh = c >> 1, c & 1
        out[b, :, dh * DH : (dh + 1) * DH] = res.results[c]["out"]
    return out


# version 08
# speedup vs baseline: 1.3084x; 1.0311x over previous
import numpy as np
import concourse.bass as bass
import concourse.mybir as mybir
from concourse.bass_utils import run_bass_kernel_spmd
from concourse.alu_op_type import AluOpType as Op

F32 = mybir.dt.float32
BF16 = mybir.dt.bfloat16
I32 = mybir.dt.int32
AF = mybir.ActivationFunctionType
AX = mybir.AxisListType

B, S, D, N, RNK, NE = 4, 2048, 4096, 16, 128, 64
DH = D // 2
TB = 512
NT = S // TB
NDC = DH // 128
NDF = D // 128
NTT = TB // 128
USE_GPSIMD = True
NT_LIM = NT
DC_LIM = NDC
A_STAGE = 6
NOACC = False


class Ctr:
    def __init__(self, nc, name):
        self.h = nc.alloc_semaphore(name)
        self.v = 0

    def inc(self, instr, k=1):
        instr.then_inc(self.h, k)
        self.v += k
        return self.v


def build_program():
    nc = bass.Bass("TRN2", target_bir_lowering=False)

    d_x = nc.dram_tensor("x", [S, D], F32, kind="ExternalInput")
    d_xh = nc.dram_tensor("xh", [S, DH], F32, kind="ExternalInput")
    d_wcat = nc.dram_tensor("wcat", [128, NDF * 224], F32, kind="ExternalInput")
    d_dtw = nc.dram_tensor("dtw", [RNK, DH], F32, kind="ExternalInput")
    d_alog = nc.dram_tensor("alog", [128, NDC * N], F32, kind="ExternalInput")
    d_dvec = nc.dram_tensor("dvec", [128, NDC], F32, kind="ExternalInput")
    d_bvec = nc.dram_tensor("bvec", [128, NDC], F32, kind="ExternalInput")
    d_rb = nc.dram_tensor("rb", [NE, 1], F32, kind="ExternalInput")
    d_eb = nc.dram_tensor("eb", [NE, DH], F32, kind="ExternalInput")
    d_out = nc.dram_tensor("out", [S, DH], F32, kind="ExternalOutput")
    d_bcs = nc.dram_tensor("bcs", [33, TB], F32)

    arena = nc.alloc_sbuf_tensor("arena", [128, NTT * D], F32)
    xnT = nc.alloc_sbuf_tensor("xnT", [128, NDC * TB], F32)
    xnT16 = nc.alloc_sbuf_tensor("xnT16", [128, 2 * TB], BF16)
    h_all = nc.alloc_sbuf_tensor("h_all", [128, N * TB], F32)
    wcat16 = nc.alloc_sbuf_tensor("wcat16", [128, NDF * 224], BF16)
    dtw16 = nc.alloc_sbuf_tensor("dtw16", [128, DH], BF16)
    eb16 = nc.alloc_sbuf_tensor("eb16", [NE, DH], BF16)
    a_sb = nc.alloc_sbuf_tensor("a_sb", [128, NDC * N], F32)
    dvec = nc.alloc_sbuf_tensor("dvec_sb", [128, NDC], F32)
    bvec = nc.alloc_sbuf_tensor("bvec_sb", [128, NDC], F32)
    rb_sb = nc.alloc_sbuf_tensor("rb_sb", [NE, 1], F32)
    wd16 = nc.alloc_sbuf_tensor("wd16", [NE, TB], BF16)
    xp16 = nc.alloc_sbuf_tensor("xp16", [128, TB], BF16)
    bc32 = nc.alloc_sbuf_tensor("bc32", [32, TB], F32)
    lgrm = nc.alloc_sbuf_tensor("lgrm", [NE, TB], F32)
    lgtm = nc.alloc_sbuf_tensor("lgtm", [128, NTT * NE], F32)
    etm = nc.alloc_sbuf_tensor("etm", [128, NTT * NE], F32)
    msk = nc.alloc_sbuf_tensor("msk", [128, NTT * NE], F32)
    wdtm = nc.alloc_sbuf_tensor("wdtm", [128, NTT * 72], F32)
    t8 = nc.alloc_sbuf_tensor("t8", [128, NTT * 8], F32)
    cols = nc.alloc_sbuf_tensor("cols", [128, 48], F32)
    spare = nc.alloc_sbuf_tensor("spare", [128, 1], F32)
    wsrow = nc.alloc_sbuf_tensor("wsrow", [1, TB], F32)
    wsb = nc.alloc_sbuf_tensor("wsb", [128, TB], F32)
    dt_sb = nc.alloc_sbuf_tensor("dt_sb", [128, 2 * TB], F32)
    sp_st = nc.alloc_sbuf_tensor("sp_st", [128, 2 * TB], F32)
    u_sb = nc.alloc_sbuf_tensor("u_sb", [128, 2 * TB], F32)
    y_sb = nc.alloc_sbuf_tensor("y_sb", [128, 2 * TB], F32)
    da_sb = nc.alloc_sbuf_tensor("da_sb", [128, 2 * TB], F32)
    db_sb = nc.alloc_sbuf_tensor("db_sb", [128, 4 * TB], F32)
    hst = nc.alloc_sbuf_tensor("hst", [128, NDC * N], F32)
    xres = nc.alloc_sbuf_tensor("xres", [128, 2 * TB], F32)
    outsb = nc.alloc_sbuf_tensor("outsb", [128, 2 * TB], F32)
    idf = nc.alloc_sbuf_tensor("idf", [128, 128], F32)
    idi = nc.alloc_sbuf_tensor("idi", [128, 128], I32)
    iopf = nc.alloc_sbuf_tensor("iopf", [32, 128], I32)
    sel = nc.alloc_sbuf_tensor("sel", [32, 2 * 128], F32)
    ones = nc.alloc_sbuf_tensor("ones", [1, 128], F32)
    eps_sb = nc.alloc_sbuf_tensor("eps_sb", [128, 1], F32)

    p0 = nc.alloc_psum_tensor("p0", [128, 512], F32)
    p1 = nc.alloc_psum_tensor("p1", [128, 512], F32)
    p2 = nc.alloc_psum_tensor("p2", [128, 512], F32)
    p3 = nc.alloc_psum_tensor("p3", [128, 512], F32)
    p4 = nc.alloc_psum_tensor("p4", [128, 512], F32)
    p5 = nc.alloc_psum_tensor("p5", [128, 512], F32)
    p6 = nc.alloc_psum_tensor("p6", [128, 512], F32)
    p7 = nc.alloc_psum_tensor("p7", [128, 512], F32)

    ld = Ctr(nc, "c_ld")
    pe = Ctr(nc, "c_pe")
    act = Ctr(nc, "c_act")
    dve = Ctr(nc, "c_dve")
    st = Ctr(nc, "c_st")
    gp = Ctr(nc, "c_gp")

    nc.all_engine_barrier()
    nc.gpsimd.sem_clear(range(ld.h.num, gp.h.num + 1))
    nc.all_engine_barrier()

    V, A, T, G, SY = nc.vector, nc.scalar, nc.tensor, nc.gpsimd, nc.sync

    ld.inc(SY.dma_start(arena[:, 0 : NDF * 224], d_wcat[:, :]), 16)
    ld.inc(SY.dma_start(h_all[:, 0:DH], d_dtw[:, :]), 16)
    ld.inc(SY.dma_start(h_all[0:NE, DH : 2 * DH], d_eb[:, :]), 16)
    ld.inc(SY.dma_start(cols[0:NE, 0:1], d_rb[:, :]), 16)
    ld.inc(SY.dma_start(a_sb[:, :], d_alog[:, :]), 16)
    ld.inc(SY.dma_start(dvec[:, :], d_dvec[:, :]), 16)
    ld.inc(SY.dma_start(bvec[:, :], d_bvec[:, :]), 16)

    G.iota(idi[:], pattern=[[1, 128]], base=0, channel_multiplier=-1)
    gp.inc(G.iota(iopf[:], pattern=[[0, 128]], base=0, channel_multiplier=1))

    V.wait_ge(ld.h, ld.v)
    V.wait_ge(gp.h, gp.v)
    V.tensor_scalar(idf[:], idi[:], 0, None, Op.is_equal)
    V.memset(ones[:], 1.0)
    V.memset(eps_sb[:], 1e-5)
    V.memset(spare[:], 0.0)
    V.memset(hst[:], 0.0)
    V.tensor_copy(wcat16[:], arena[:, 0 : NDF * 224])
    V.tensor_copy(dtw16[:], h_all[:, 0:DH])
    V.tensor_copy(eb16[:], h_all[0:NE, DH : 2 * DH])
    V.tensor_copy(rb_sb[:], cols[0:NE, 0:1])
    mv = dve.inc(V.engine_nop())
    A.wait_ge(dve.h, mv)
    ma = act.inc(A.activation(a_sb[:], a_sb[:], AF.Exp, bias=0.0, scale=1.0))
    V.wait_ge(act.h, ma)
    V.tensor_scalar(a_sb[:], a_sb[:], -1.0, None, Op.mult)
    dve.inc(V.engine_nop())

    nc.all_engine_barrier()

    m_gp = {}
    m_scan = {}
    m_sp = {}
    m_u = {}
    m_ydone = {}
    m_bm = {}
    m_dtmm = {}
    m_ytr = {}
    m_oadd = {}
    m_st = {}
    for tau in range(NT_LIM):
        r0 = tau * TB
        for tt in range(NTT):
            ld.inc(SY.dma_start(arena[:, tt * D : (tt + 1) * D],
                                d_x[r0 + tt * 128 : r0 + (tt + 1) * 128, :]), 16)
        m_ld_x = ld.v
        V.wait_ge(ld.h, m_ld_x)
        if A_STAGE < 2:
            nc.all_engine_barrier()
            continue
        XT = [arena[:, tt * D : (tt + 1) * D] for tt in range(NTT)]
        cl = lambda j, tt: cols[:, 4 * j + tt : 4 * j + tt + 1]
        for tt in range(NTT):
            V.tensor_reduce(cl(0, tt), XT[tt], AX.X, Op.add)
        for tt in range(NTT):
            if NOACC:
                V.tensor_reduce(cl(1, tt), XT[tt], AX.X, Op.add)
            else:
                V.scalar_tensor_tensor(h_all[:, (tt & 1) * D : (tt & 1) * D + D],
                                       XT[tt], 1.0, XT[tt],
                                       Op.mult, Op.mult, accum_out=cl(1, tt))
        for tt in range(NTT):
            V.tensor_scalar(cl(2, tt), cl(0, tt), 1.0 / D, None, Op.mult)
        for tt in range(NTT):
            V.tensor_scalar(cl(3, tt), cl(1, tt), 1.0 / D, None, Op.mult)
        for tt in range(NTT):
            V.scalar_tensor_tensor(cl(4, tt), cl(2, tt), 1.0, cl(2, tt),
                                   Op.mult, Op.mult)
        for tt in range(NTT):
            V.tensor_tensor(cl(5, tt), cl(3, tt), cl(4, tt), Op.subtract)
        m1 = dve.inc(V.engine_nop())
        if A_STAGE < 3:
            nc.all_engine_barrier()
            continue
        A.wait_ge(dve.h, m1)
        for tt in range(NTT):
            A.activation(cl(6, tt), cl(5, tt), AF.Ln, bias=eps_sb[:, 0:1],
                         scale=1.0)
        m2 = act.v
        for tt in range(NTT):
            m2 = act.inc(A.activation(cl(7, tt), cl(6, tt), AF.Exp, bias=0.0,
                                      scale=-0.5))
        V.wait_ge(act.h, m2)
        for tt in range(NTT):
            V.scalar_tensor_tensor(cl(8, tt), cl(2, tt), -1.0, cl(7, tt),
                                   Op.mult, Op.mult)
        for tt in range(NTT):
            V.tensor_scalar(XT[tt], XT[tt], cl(7, tt), cl(8, tt), Op.mult, Op.add)
        m_xn = dve.inc(V.engine_nop())
        if A_STAGE < 4:
            nc.all_engine_barrier()
            continue

        T.wait_ge(dve.h, m_xn)
        m_tr = {}
        m_evf = {}
        def emit_trans(dc):
            ptp = (p0, p1)[dc & 1]
            if dc >= 2:
                T.wait_ge(dve.h, m_evf[dc - 2])
            for tt in range(NTT):
                ti = T.transpose(ptp[:, tt * 128 : (tt + 1) * 128],
                                 arena[:, tt * D + dc * 128 : tt * D + (dc + 1) * 128],
                                 idf[:])
            m_tr[dc] = pe.inc(ti)
        emit_trans(0)
        for dc in range(NDF):
            slot = dc & 1
            ptp = (p0, p1)[slot]
            V.wait_ge(pe.h, m_tr[dc])
            V.tensor_copy(xnT16[:, slot * TB : (slot + 1) * TB], ptp[:])
            if dc < NDC:
                V.tensor_copy(xnT[:, dc * TB : (dc + 1) * TB], ptp[:])
            m_evf[dc] = dve.inc(V.engine_nop())
            if dc + 1 < NDF:
                emit_trans(dc + 1)
            T.wait_ge(dve.h, m_evf[dc])
            w = wcat16.rearrange("p (j c) -> p j c", j=NDF)[:, dc, :]
            rhs = xnT16[:, slot * TB : (slot + 1) * TB]
            T.matmul(p2[:], w[:, 0:128], rhs, start=(dc == 0), stop=(dc == NDF - 1))
            T.matmul(p3[0:32, :], w[:, 128:160], rhs, start=(dc == 0),
                     stop=(dc == NDF - 1))
            mi = T.matmul(p4[0:NE, :], w[:, 160:224], rhs, start=(dc == 0),
                          stop=(dc == NDF - 1))
        m5 = pe.inc(mi)
        A.wait_ge(pe.h, m5)
        A.activation(xp16[:], p2[:], AF.Copy, bias=0.0, scale=1.0)
        A.activation(bc32[:], p3[0:32, :], AF.Copy, bias=0.0, scale=1.0)
        m7 = act.inc(A.activation(lgrm[:], p4[0:NE, :], AF.Identity,
                                  bias=rb_sb[:, 0:1], scale=1.0))
        if A_STAGE < 5:
            nc.all_engine_barrier()
            continue
        T.wait_ge(act.h, m7)
        for tt in range(NTT):
            ti = T.transpose(p5[:, tt * NE : (tt + 1) * NE],
                             lgrm[:, tt * 128 : (tt + 1) * 128], idf[0:NE, 0:NE])
        m8 = pe.inc(ti)
        V.wait_ge(pe.h, m8)
        V.tensor_copy(lgtm[:], p5[:, 0 : NTT * NE])
        E_ = lambda tt: etm[:, tt * NE : (tt + 1) * NE]
        L_ = lambda tt: lgtm[:, tt * NE : (tt + 1) * NE]
        M_ = lambda tt: msk[:, tt * NE : (tt + 1) * NE]
        T8 = lambda tt: t8[:, tt * 8 : (tt + 1) * 8]
        W_ = lambda tt: wdtm[:, tt * 72 : tt * 72 + NE + 1]
        for tt in range(NTT):
            V.tensor_reduce(cl(0, tt), L_(tt), AX.X, Op.max)
        for tt in range(NTT):
            V.tensor_scalar(cl(1, tt), cl(0, tt), -1.0, None, Op.mult)
        m9 = dve.inc(V.engine_nop())
        A.wait_ge(dve.h, m9)
        m10 = act.v
        for tt in range(NTT):
            m10 = act.inc(A.activation(E_(tt), L_(tt), AF.Exp,
                                       bias=cl(1, tt), scale=1.0))
        V.wait_ge(act.h, m10)
        for tt in range(NTT):
            V.tensor_reduce(cl(2, tt), E_(tt), AX.X, Op.add)
        for tt in range(NTT):
            V.max(T8(tt), E_(tt))
        for tt in range(NTT):
            V.reciprocal(cl(3, tt), cl(2, tt))
        for tt in range(NTT):
            V.tensor_scalar(M_(tt), E_(tt), t8[:, tt * 8 + 7 : tt * 8 + 8],
                            None, Op.is_ge)
        for tt in range(NTT):
            V.tensor_reduce(cl(4, tt), T8(tt), AX.X, Op.add)
        for tt in range(NTT):
            V.tensor_tensor(wdtm[:, tt * 72 + NE : tt * 72 + NE + 1],
                            cl(4, tt), cl(3, tt), Op.mult)
        for tt in range(NTT):
            V.scalar_tensor_tensor(wdtm[:, tt * 72 : tt * 72 + NE],
                                   E_(tt), cl(3, tt), M_(tt), Op.mult, Op.mult)
        m11 = dve.inc(V.engine_nop())
        T.wait_ge(dve.h, m11)
        for tt in range(NTT):
            ti = T.transpose(p5[0 : NE + 1, tt * 128 : tt * 128 + 128],
                             W_(tt), idf[:])
        m12 = pe.inc(ti)
        A.wait_ge(pe.h, m12)
        for tt in range(NTT):
            A.activation(wd16[:, tt * 128 : (tt + 1) * 128],
                         p5[0:NE, tt * 128 : tt * 128 + 128], AF.Copy,
                         bias=0.0, scale=1.0)
        m12b = act.v
        for tt in range(NTT):
            m12b = act.inc(A.activation(wsrow[0:1, tt * 128 : (tt + 1) * 128],
                                        p5[NE : NE + 1, tt * 128 : tt * 128 + 128],
                                        AF.Copy, bias=0.0, scale=1.0))
        if A_STAGE < 6:
            nc.all_engine_barrier()
            continue
        SY.wait_ge(act.h, m12b)
        ld.inc(SY.dma_start(d_bcs[0:32, :], bc32[:, :]), 16)
        ld.inc(SY.dma_start(d_bcs[32:33, :], wsrow[0:1, :]), 16)
        SY.wait_ge(ld.h, ld.v)
        SY.wait_ge(pe.h, m_tr[NDF - 1])
        for n in range(32):
            ld.inc(SY.dma_start(arena[:, n * TB : (n + 1) * TB],
                                bass.AP(d_bcs, n * TB, [[0, 128], [1, TB]])), 16)
        ld.inc(SY.dma_start(wsb[:, :],
                            bass.AP(d_bcs, 32 * TB, [[0, 128], [1, TB]])), 16)
        SY.wait_ge(ld.h, ld.v)

        nc.all_engine_barrier()

        for dc in range(DC_LIM):
            g = tau * NDC + dc
            sl2 = dc & 1
            pdt = (p0, p1)[sl2]
            pbm = (p2, p3)[sl2]
            ptpy = (p4, p5)[sl2]
            tcol = slice(dc * TB, (dc + 1) * TB)
            dsl = slice(sl2 * TB, (sl2 + 1) * TB)

            if g >= 2:
                T.wait_ge(act.h, m_sp[g - 2])
            mi = T.matmul(pdt[:], dtw16[:, dc * 128 : (dc + 1) * 128], xp16[:],
                          start=True, stop=True)
            m_dtmm[g] = pe.inc(mi)
            if g >= 2:
                T.wait_ge(dve.h, m_ydone[g - 2])
            mi = T.matmul(pbm[:], eb16[:, dc * 128 : (dc + 1) * 128], wd16[:],
                          start=True, stop=True)
            m_bm[g] = pe.inc(mi)

            A.wait_ge(pe.h, m_dtmm[g])
            if g >= 2:
                A.wait_ge(dve.h, m_u[g - 2])
            A.activation(sp_st[:, dsl], pdt[:], AF.Exp,
                         bias=bvec[:, dc : dc + 1], scale=1.0)
            m_sp[g] = act.inc(A.activation(dt_sb[:, dsl], sp_st[:, dsl],
                                            AF.Ln, bias=1.0, scale=1.0))

            V.wait_ge(act.h, m_sp[g])
            if g >= 2:
                V.wait_ge(pe.h, m_ytr.get(g - 2, 0))
                if USE_GPSIMD:
                    V.wait_ge(gp.h, m_gp[(g - 2, N - 1)])
            V.tensor_tensor(u_sb[:, dsl], dt_sb[:, dsl], xnT[:, tcol], Op.mult)
            V.scalar_tensor_tensor(y_sb[:, dsl], wsb[:], dvec[:, dc : dc + 1],
                                   xnT[:, tcol], Op.add, Op.mult)
            V.tensor_tensor(db_sb[:, 0:TB], u_sb[:, dsl], arena[:, 0:TB], Op.mult)
            m_u[g] = dve.inc(V.engine_nop())

            NGP = 16 if USE_GPSIMD else 0
            if USE_GPSIMD:
                G.wait_ge(dve.h, m_u[g])
                for k in (0, 1):
                    if (g - 1, k + 12) in m_scan:
                        G.wait_ge(dve.h, m_scan[(g - 1, k + 12)])
                    gi = G.tensor_tensor(db_sb[:, (k & 3) * TB : (k & 3) * TB + TB],
                                         u_sb[:, dsl],
                                         arena[:, k * TB : (k + 1) * TB], Op.mult)
                    m_gp[(g, k)] = gp.inc(gi)

            for n in range(N):
                par = n & 1
                pda = (p6, p7)[par]
                if n >= 2:
                    A.wait_ge(dve.h, m_scan[(g, n - 2)])
                else:
                    A.wait_ge(dve.h, m_u[g])
                m_da = act.inc(A.activation(
                    pda[:], dt_sb[:, dsl], AF.Exp, bias=0.0,
                    scale=a_sb[:, dc * N + n : dc * N + n + 1]))
                k = n + 2
                if k < N:
                    if k < NGP:
                        G.wait_ge(dve.h, m_scan[(g, k - 4)] if k >= 4 else
                                  m_u[g])
                        gi = G.tensor_tensor(
                            db_sb[:, (k & 3) * TB : (k & 3) * TB + TB],
                            u_sb[:, dsl], arena[:, k * TB : (k + 1) * TB],
                            Op.mult)
                        m_gp[(g, k)] = gp.inc(gi)
                    else:
                        V.tensor_tensor(
                            db_sb[:, (k & 3) * TB : (k & 3) * TB + TB],
                            u_sb[:, dsl], arena[:, k * TB : (k + 1) * TB],
                            Op.mult)
                if not USE_GPSIMD and n == 1:
                    V.tensor_tensor(db_sb[:, (n & 3) * TB : (n & 3) * TB + TB],
                                    u_sb[:, dsl],
                                    arena[:, n * TB : (n + 1) * TB], Op.mult)
                V.wait_ge(act.h, m_da)
                if n < NGP:
                    V.wait_ge(gp.h, m_gp[(g, n)])
                V.tensor_tensor_scan(h_all[:, n * TB : (n + 1) * TB],
                                     pda[:],
                                     db_sb[:, (n & 3) * TB : (n & 3) * TB + TB],
                                     hst[:, dc * N + n : dc * N + n + 1],
                                     Op.mult, Op.add)
                m_scan[(g, n)] = dve.inc(V.engine_nop())

            V.engine_nop()
            V.tensor_copy(hst[:, dc * N : (dc + 1) * N],
                          h_all.rearrange("p (n t) -> p n t", n=N)[:, :, TB - 1])
            V.tensor_tensor(h_all[:], h_all[:], arena[:, 16 * TB : 32 * TB],
                            Op.mult)
            w = 8
            while w >= 1:
                V.tensor_tensor(h_all[:, 0 : w * TB], h_all[:, 0 : w * TB],
                                h_all[:, w * TB : 2 * w * TB], Op.add)
                w //= 2
            V.tensor_tensor(y_sb[:, dsl], y_sb[:, dsl], h_all[:, 0:TB], Op.add)
            V.wait_ge(pe.h, m_bm[g])
            V.tensor_tensor(y_sb[:, dsl], y_sb[:, dsl], pbm[:], Op.add)
            m_ydone[g] = dve.inc(V.engine_nop())

            if (g - 2) in m_oadd:
                SY.wait_ge(dve.h, m_oadd[g - 2])
            for tt in range(NTT):
                ld.inc(SY.dma_start(
                    xres[:, sl2 * TB + tt * 128 : sl2 * TB + (tt + 1) * 128],
                    bass.AP(d_xh, (r0 + tt * 128) * DH + dc * 128,
                            [[DH, 128], [1, 128]])), 16)
            m_ld_res = ld.v

            T.wait_ge(dve.h, m_ydone[g])
            if (g - 2) in m_oadd:
                T.wait_ge(dve.h, m_oadd[g - 2])
            for k in range(4):
                ti = T.transpose(
                    ptpy[:, k * 128 : (k + 1) * 128],
                    y_sb[:, sl2 * TB + k * 128 : sl2 * TB + (k + 1) * 128],
                    idf[:])
            m_ytr[g] = pe.inc(ti)
            V.wait_ge(pe.h, m_ytr[g])
            V.wait_ge(ld.h, m_ld_res)
            if (g - 2) in m_st:
                V.wait_ge(st.h, m_st[g - 2])
            V.tensor_tensor(outsb[:, dsl], xres[:, dsl], ptpy[:], Op.add)
            m_oadd[g] = dve.inc(V.engine_nop())
            SY.wait_ge(dve.h, m_oadd[g])
            for tt in range(NTT):
                st.inc(SY.dma_start(
                    bass.AP(d_out, (r0 + tt * 128) * DH + dc * 128,
                            [[DH, 128], [1, 128]]),
                    outsb[:, sl2 * TB + tt * 128 : sl2 * TB + (tt + 1) * 128]), 16)
            m_st[g] = st.v

        nc.all_engine_barrier()

    SY.wait_ge(st.h, st.v)
    SY.wait_ge(ld.h, ld.v)
    nc.all_engine_barrier()
    return nc


def kernel(L0_attn_x, L0_attn_A_log, L0_attn_D, L0_attn_x_proj_w,
           L0_attn_dt_proj_w, L0_attn_dt_proj_b, L0_ffn_router_weight,
           L0_ffn_router_bias, L0_ffn_expert_bias):
    x = np.asarray(L0_attn_x, dtype=np.float32)
    wcat_rows = np.concatenate([np.asarray(L0_attn_x_proj_w, np.float32),
                                np.asarray(L0_ffn_router_weight, np.float32)],
                               axis=0).T
    in_maps = []
    for c in range(8):
        b, dh = c >> 1, c & 1
        sl = slice(dh * DH, (dh + 1) * DH)
        perm = np.r_[dh * DH : (dh + 1) * DH, (1 - dh) * DH : (2 - dh) * DH]
        wc = wcat_rows[perm]
        wc = np.ascontiguousarray(
            wc.reshape(NDF, 128, 224).transpose(1, 0, 2).reshape(128, NDF * 224))
        in_maps.append(dict(
            x=np.ascontiguousarray(x[b][:, perm]),
            xh=np.ascontiguousarray(x[b][:, sl]),
            wcat=wc,
            dtw=np.ascontiguousarray(np.asarray(L0_attn_dt_proj_w, np.float32)[sl].T),
            alog=np.ascontiguousarray(
                np.asarray(L0_attn_A_log, np.float32)[sl]
                .reshape(NDC, 128, N).transpose(1, 0, 2).reshape(128, NDC * N)),
            dvec=np.ascontiguousarray(
                np.asarray(L0_attn_D, np.float32)[sl].reshape(NDC, 128).T),
            bvec=np.ascontiguousarray(
                np.asarray(L0_attn_dt_proj_b, np.float32)[sl].reshape(NDC, 128).T),
            rb=np.ascontiguousarray(
                np.asarray(L0_ffn_router_bias, np.float32)[:, None]),
            eb=np.ascontiguousarray(
                np.asarray(L0_ffn_expert_bias, np.float32)[:, sl]),
        ))
    nc = build_program()
    res = run_bass_kernel_spmd(nc, in_maps, core_ids=list(range(8)))
    out = np.empty((B, S, D), dtype=np.float32)
    for c in range(8):
        b, dh = c >> 1, c & 1
        out[b, :, dh * DH : (dh + 1) * DH] = res.results[c]["out"]
    return out


# version 09
# speedup vs baseline: 1.3500x; 1.0317x over previous
import numpy as np
import concourse.bass as bass
import concourse.mybir as mybir
from concourse.bass_utils import run_bass_kernel_spmd
from concourse.alu_op_type import AluOpType as Op

F32 = mybir.dt.float32
BF16 = mybir.dt.bfloat16
I32 = mybir.dt.int32
AF = mybir.ActivationFunctionType
AX = mybir.AxisListType

B, S, D, N, RNK, NE = 4, 2048, 4096, 16, 128, 64
DH = D // 2
TB = 512
NT = S // TB
NDC = DH // 128
NDF = D // 128
NTT = TB // 128
USE_GPSIMD = True
NT_LIM = NT
DC_LIM = NDC
A_STAGE = 6
NOACC = False


class Ctr:
    def __init__(self, nc, name):
        self.h = nc.alloc_semaphore(name)
        self.v = 0

    def inc(self, instr, k=1):
        instr.then_inc(self.h, k)
        self.v += k
        return self.v


def build_program():
    nc = bass.Bass("TRN2", target_bir_lowering=False)

    d_x = nc.dram_tensor("x", [S, D], F32, kind="ExternalInput")
    d_xh = nc.dram_tensor("xh", [S, DH], F32, kind="ExternalInput")
    d_wcat = nc.dram_tensor("wcat", [128, NDF * 224], F32, kind="ExternalInput")
    d_dtw = nc.dram_tensor("dtw", [RNK, DH], F32, kind="ExternalInput")
    d_alog = nc.dram_tensor("alog", [128, NDC * N], F32, kind="ExternalInput")
    d_dvec = nc.dram_tensor("dvec", [128, NDC], F32, kind="ExternalInput")
    d_bvec = nc.dram_tensor("bvec", [128, NDC], F32, kind="ExternalInput")
    d_rb = nc.dram_tensor("rb", [NE, 1], F32, kind="ExternalInput")
    d_eb = nc.dram_tensor("eb", [NE, DH], F32, kind="ExternalInput")
    d_out = nc.dram_tensor("out", [S, DH], F32, kind="ExternalOutput")
    d_bcs = nc.dram_tensor("bcs", [33, TB], F32)

    arena = nc.alloc_sbuf_tensor("arena", [128, NTT * D], F32)
    xnT = nc.alloc_sbuf_tensor("xnT", [128, NDC * TB], F32)
    xnT16 = nc.alloc_sbuf_tensor("xnT16", [128, 2 * TB], BF16)
    h_all = nc.alloc_sbuf_tensor("h_all", [128, N * TB], F32)
    wcat16 = nc.alloc_sbuf_tensor("wcat16", [128, NDF * 224], BF16)
    dtw16 = nc.alloc_sbuf_tensor("dtw16", [128, DH], BF16)
    eb16 = nc.alloc_sbuf_tensor("eb16", [NE, DH], BF16)
    a_sb = nc.alloc_sbuf_tensor("a_sb", [128, NDC * N], F32)
    dvec = nc.alloc_sbuf_tensor("dvec_sb", [128, NDC], F32)
    bvec = nc.alloc_sbuf_tensor("bvec_sb", [128, NDC], F32)
    rb_sb = nc.alloc_sbuf_tensor("rb_sb", [NE, 1], F32)
    wd16 = nc.alloc_sbuf_tensor("wd16", [NE, TB], BF16)
    xp16 = nc.alloc_sbuf_tensor("xp16", [128, TB], BF16)
    bc32 = nc.alloc_sbuf_tensor("bc32", [32, TB], F32)
    lgrm = nc.alloc_sbuf_tensor("lgrm", [NE, TB], F32)
    lgtm = nc.alloc_sbuf_tensor("lgtm", [128, NTT * NE], F32)
    etm = nc.alloc_sbuf_tensor("etm", [128, NTT * NE], F32)
    msk = nc.alloc_sbuf_tensor("msk", [128, NTT * NE], F32)
    wdtm = nc.alloc_sbuf_tensor("wdtm", [128, NTT * 72], F32)
    t8 = nc.alloc_sbuf_tensor("t8", [128, NTT * 8], F32)
    cols = nc.alloc_sbuf_tensor("cols", [128, 48], F32)
    spare = nc.alloc_sbuf_tensor("spare", [128, 1], F32)
    wsrow = nc.alloc_sbuf_tensor("wsrow", [1, TB], F32)
    wsb = nc.alloc_sbuf_tensor("wsb", [128, TB], F32)
    dt_sb = nc.alloc_sbuf_tensor("dt_sb", [128, 2 * TB], F32)
    sp_st = nc.alloc_sbuf_tensor("sp_st", [128, 2 * TB], F32)
    u_sb = nc.alloc_sbuf_tensor("u_sb", [128, 2 * TB], F32)
    y_sb = nc.alloc_sbuf_tensor("y_sb", [128, 2 * TB], F32)
    da_sb = nc.alloc_sbuf_tensor("da_sb", [128, 2 * TB], F32)
    db_sb = nc.alloc_sbuf_tensor("db_sb", [128, 4 * TB], F32)
    hst = nc.alloc_sbuf_tensor("hst", [128, NDC * N], F32)
    xres = nc.alloc_sbuf_tensor("xres", [128, 2 * TB], F32)
    outsb = nc.alloc_sbuf_tensor("outsb", [128, 2 * TB], F32)
    idf = nc.alloc_sbuf_tensor("idf", [128, 128], F32)
    idi = nc.alloc_sbuf_tensor("idi", [128, 128], I32)
    iopf = nc.alloc_sbuf_tensor("iopf", [32, 128], I32)
    sel = nc.alloc_sbuf_tensor("sel", [32, 2 * 128], F32)
    ones = nc.alloc_sbuf_tensor("ones", [1, 128], F32)
    eps_sb = nc.alloc_sbuf_tensor("eps_sb", [128, 1], F32)

    p0 = nc.alloc_psum_tensor("p0", [128, 512], F32)
    p1 = nc.alloc_psum_tensor("p1", [128, 512], F32)
    p2 = nc.alloc_psum_tensor("p2", [128, 512], F32)
    p3 = nc.alloc_psum_tensor("p3", [128, 512], F32)
    p4 = nc.alloc_psum_tensor("p4", [128, 512], F32)
    p5 = nc.alloc_psum_tensor("p5", [128, 512], F32)
    p6 = nc.alloc_psum_tensor("p6", [128, 512], F32)
    p7 = nc.alloc_psum_tensor("p7", [128, 512], F32)

    ld = Ctr(nc, "c_ld")
    pe = Ctr(nc, "c_pe")
    act = Ctr(nc, "c_act")
    dve = Ctr(nc, "c_dve")
    st = Ctr(nc, "c_st")
    gp = Ctr(nc, "c_gp")

    nc.all_engine_barrier()
    nc.gpsimd.sem_clear(range(ld.h.num, gp.h.num + 1))
    nc.all_engine_barrier()

    V, A, T, G, SY = nc.vector, nc.scalar, nc.tensor, nc.gpsimd, nc.sync

    ld.inc(SY.dma_start(arena[:, 0 : NDF * 224], d_wcat[:, :]), 16)
    ld.inc(SY.dma_start(h_all[:, 0:DH], d_dtw[:, :]), 16)
    ld.inc(SY.dma_start(h_all[0:NE, DH : 2 * DH], d_eb[:, :]), 16)
    ld.inc(SY.dma_start(cols[0:NE, 0:1], d_rb[:, :]), 16)
    ld.inc(SY.dma_start(a_sb[:, :], d_alog[:, :]), 16)
    ld.inc(SY.dma_start(dvec[:, :], d_dvec[:, :]), 16)
    ld.inc(SY.dma_start(bvec[:, :], d_bvec[:, :]), 16)

    G.iota(idi[:], pattern=[[1, 128]], base=0, channel_multiplier=-1)
    gp.inc(G.iota(iopf[:], pattern=[[0, 128]], base=0, channel_multiplier=1))

    V.wait_ge(ld.h, ld.v)
    V.wait_ge(gp.h, gp.v)
    V.tensor_scalar(idf[:], idi[:], 0, None, Op.is_equal)
    V.memset(ones[:], 1.0)
    V.memset(eps_sb[:], 1e-5)
    V.memset(spare[:], 0.0)
    V.memset(hst[:], 0.0)
    V.tensor_copy(wcat16[:], arena[:, 0 : NDF * 224])
    V.tensor_copy(dtw16[:], h_all[:, 0:DH])
    V.tensor_copy(eb16[:], h_all[0:NE, DH : 2 * DH])
    V.tensor_copy(rb_sb[:], cols[0:NE, 0:1])
    mv = dve.inc(V.engine_nop())
    A.wait_ge(dve.h, mv)
    ma = act.inc(A.activation(a_sb[:], a_sb[:], AF.Exp, bias=0.0, scale=1.0))
    V.wait_ge(act.h, ma)
    V.tensor_scalar(a_sb[:], a_sb[:], -1.0, None, Op.mult)
    dve.inc(V.engine_nop())

    nc.all_engine_barrier()

    m_gp = {}
    m_scan = {}
    m_sp = {}
    m_u = {}
    m_ydone = {}
    m_bm = {}
    m_dtmm = {}
    m_ytr = {}
    m_oadd = {}
    m_st = {}
    for tau in range(NT_LIM):
        r0 = tau * TB
        for tt in range(NTT):
            ld.inc(SY.dma_start(arena[:, tt * D : (tt + 1) * D],
                                d_x[r0 + tt * 128 : r0 + (tt + 1) * 128, :]), 16)
        m_ld_x = ld.v
        V.wait_ge(ld.h, m_ld_x)
        if A_STAGE < 2:
            nc.all_engine_barrier()
            continue
        XT = [arena[:, tt * D : (tt + 1) * D] for tt in range(NTT)]
        cl = lambda j, tt: cols[:, 4 * j + tt : 4 * j + tt + 1]
        for tt in range(NTT):
            V.tensor_reduce(cl(0, tt), XT[tt], AX.X, Op.add)
        for tt in range(NTT):
            if NOACC:
                V.tensor_reduce(cl(1, tt), XT[tt], AX.X, Op.add)
            else:
                V.scalar_tensor_tensor(h_all[:, (tt & 1) * D : (tt & 1) * D + D],
                                       XT[tt], 1.0, XT[tt],
                                       Op.mult, Op.mult, accum_out=cl(1, tt))
        for tt in range(NTT):
            V.tensor_scalar(cl(2, tt), cl(0, tt), 1.0 / D, None, Op.mult)
        for tt in range(NTT):
            V.tensor_scalar(cl(3, tt), cl(1, tt), 1.0 / D, None, Op.mult)
        for tt in range(NTT):
            V.scalar_tensor_tensor(cl(4, tt), cl(2, tt), 1.0, cl(2, tt),
                                   Op.mult, Op.mult)
        for tt in range(NTT):
            V.tensor_tensor(cl(5, tt), cl(3, tt), cl(4, tt), Op.subtract)
        m1 = dve.inc(V.engine_nop())
        if A_STAGE < 3:
            nc.all_engine_barrier()
            continue
        A.wait_ge(dve.h, m1)
        for tt in range(NTT):
            A.activation(cl(6, tt), cl(5, tt), AF.Ln, bias=eps_sb[:, 0:1],
                         scale=1.0)
        m2 = act.v
        for tt in range(NTT):
            m2 = act.inc(A.activation(cl(7, tt), cl(6, tt), AF.Exp, bias=0.0,
                                      scale=-0.5))
        V.wait_ge(act.h, m2)
        for tt in range(NTT):
            V.scalar_tensor_tensor(cl(8, tt), cl(2, tt), -1.0, cl(7, tt),
                                   Op.mult, Op.mult)
        for tt in range(NTT):
            V.tensor_scalar(XT[tt], XT[tt], cl(7, tt), cl(8, tt), Op.mult, Op.add)
        m_xn = dve.inc(V.engine_nop())
        if A_STAGE < 4:
            nc.all_engine_barrier()
            continue

        T.wait_ge(dve.h, m_xn)
        m_tr = {}
        m_evf = {}
        def emit_trans(dc):
            ptp = (p0, p1)[dc & 1]
            if dc >= 2:
                T.wait_ge(dve.h, m_evf[dc - 2])
            for tt in range(NTT):
                ti = T.transpose(ptp[:, tt * 128 : (tt + 1) * 128],
                                 arena[:, tt * D + dc * 128 : tt * D + (dc + 1) * 128],
                                 idf[:])
            m_tr[dc] = pe.inc(ti)
        emit_trans(0)
        for dc in range(NDF):
            slot = dc & 1
            ptp = (p0, p1)[slot]
            V.wait_ge(pe.h, m_tr[dc])
            V.tensor_copy(xnT16[:, slot * TB : (slot + 1) * TB], ptp[:])
            if dc < NDC:
                V.tensor_copy(xnT[:, dc * TB : (dc + 1) * TB], ptp[:])
            m_evf[dc] = dve.inc(V.engine_nop())
            if dc + 1 < NDF:
                emit_trans(dc + 1)
            T.wait_ge(dve.h, m_evf[dc])
            w = wcat16.rearrange("p (j c) -> p j c", j=NDF)[:, dc, :]
            rhs = xnT16[:, slot * TB : (slot + 1) * TB]
            T.matmul(p2[:], w[:, 0:128], rhs, start=(dc == 0), stop=(dc == NDF - 1))
            T.matmul(p3[0:32, :], w[:, 128:160], rhs, start=(dc == 0),
                     stop=(dc == NDF - 1))
            mi = T.matmul(p4[0:NE, :], w[:, 160:224], rhs, start=(dc == 0),
                          stop=(dc == NDF - 1))
        m5 = pe.inc(mi)
        A.wait_ge(pe.h, m5)
        A.activation(xp16[:], p2[:], AF.Copy, bias=0.0, scale=1.0)
        A.activation(bc32[:], p3[0:32, :], AF.Copy, bias=0.0, scale=1.0)
        m7 = act.inc(A.activation(lgrm[:], p4[0:NE, :], AF.Identity,
                                  bias=rb_sb[:, 0:1], scale=1.0))
        if A_STAGE < 5:
            nc.all_engine_barrier()
            continue
        T.wait_ge(act.h, m7)
        for tt in range(NTT):
            ti = T.transpose(p5[:, tt * NE : (tt + 1) * NE],
                             lgrm[:, tt * 128 : (tt + 1) * 128], idf[0:NE, 0:NE])
        m8 = pe.inc(ti)
        V.wait_ge(pe.h, m8)
        V.tensor_copy(lgtm[:], p5[:, 0 : NTT * NE])
        E_ = lambda tt: etm[:, tt * NE : (tt + 1) * NE]
        L_ = lambda tt: lgtm[:, tt * NE : (tt + 1) * NE]
        M_ = lambda tt: msk[:, tt * NE : (tt + 1) * NE]
        T8 = lambda tt: t8[:, tt * 8 : (tt + 1) * 8]
        W_ = lambda tt: wdtm[:, tt * 72 : tt * 72 + NE + 1]
        for tt in range(NTT):
            V.tensor_reduce(cl(0, tt), L_(tt), AX.X, Op.max)
        for tt in range(NTT):
            V.tensor_scalar(cl(1, tt), cl(0, tt), -1.0, None, Op.mult)
        m9 = dve.inc(V.engine_nop())
        A.wait_ge(dve.h, m9)
        m10 = act.v
        for tt in range(NTT):
            m10 = act.inc(A.activation(E_(tt), L_(tt), AF.Exp,
                                       bias=cl(1, tt), scale=1.0))
        V.wait_ge(act.h, m10)
        for tt in range(NTT):
            V.tensor_reduce(cl(2, tt), E_(tt), AX.X, Op.add)
        for tt in range(NTT):
            V.max(T8(tt), E_(tt))
        for tt in range(NTT):
            V.reciprocal(cl(3, tt), cl(2, tt))
        for tt in range(NTT):
            V.tensor_scalar(M_(tt), E_(tt), t8[:, tt * 8 + 7 : tt * 8 + 8],
                            None, Op.is_ge)
        for tt in range(NTT):
            V.tensor_reduce(cl(4, tt), T8(tt), AX.X, Op.add)
        for tt in range(NTT):
            V.tensor_tensor(wdtm[:, tt * 72 + NE : tt * 72 + NE + 1],
                            cl(4, tt), cl(3, tt), Op.mult)
        for tt in range(NTT):
            V.scalar_tensor_tensor(wdtm[:, tt * 72 : tt * 72 + NE],
                                   E_(tt), cl(3, tt), M_(tt), Op.mult, Op.mult)
        m11 = dve.inc(V.engine_nop())
        T.wait_ge(dve.h, m11)
        for tt in range(NTT):
            ti = T.transpose(p5[0 : NE + 1, tt * 128 : tt * 128 + 128],
                             W_(tt), idf[:])
        m12 = pe.inc(ti)
        A.wait_ge(pe.h, m12)
        for tt in range(NTT):
            A.activation(wd16[:, tt * 128 : (tt + 1) * 128],
                         p5[0:NE, tt * 128 : tt * 128 + 128], AF.Copy,
                         bias=0.0, scale=1.0)
        m12b = act.v
        for tt in range(NTT):
            m12b = act.inc(A.activation(wsrow[0:1, tt * 128 : (tt + 1) * 128],
                                        p5[NE : NE + 1, tt * 128 : tt * 128 + 128],
                                        AF.Copy, bias=0.0, scale=1.0))
        if A_STAGE < 6:
            nc.all_engine_barrier()
            continue
        SY.wait_ge(act.h, m12b)
        ld.inc(SY.dma_start(d_bcs[0:32, :], bc32[:, :]), 16)
        ld.inc(SY.dma_start(d_bcs[32:33, :], wsrow[0:1, :]), 16)
        SY.wait_ge(ld.h, ld.v)
        SY.wait_ge(pe.h, m_tr[NDF - 1])
        for n in range(32):
            ld.inc(SY.dma_start(arena[:, n * TB : (n + 1) * TB],
                                bass.AP(d_bcs, n * TB, [[0, 128], [1, TB]])), 16)
        ld.inc(SY.dma_start(wsb[:, :],
                            bass.AP(d_bcs, 32 * TB, [[0, 128], [1, TB]])), 16)
        SY.wait_ge(ld.h, ld.v)

        nc.all_engine_barrier()

        for dc in range(DC_LIM):
            g = tau * NDC + dc
            sl2 = dc & 1
            pdt = p0
            pbm = p2
            ptpy = (p4, p5)[sl2]
            tcol = slice(dc * TB, (dc + 1) * TB)
            dsl = slice(sl2 * TB, (sl2 + 1) * TB)

            if g >= 1:
                T.wait_ge(act.h, m_sp[g - 1])
            mi = T.matmul(pdt[:], dtw16[:, dc * 128 : (dc + 1) * 128], xp16[:],
                          start=True, stop=True)
            m_dtmm[g] = pe.inc(mi)
            if g >= 1:
                T.wait_ge(dve.h, m_ydone[g - 1])
            mi = T.matmul(pbm[:], eb16[:, dc * 128 : (dc + 1) * 128], wd16[:],
                          start=True, stop=True)
            m_bm[g] = pe.inc(mi)

            A.wait_ge(pe.h, m_dtmm[g])
            if g >= 2:
                A.wait_ge(dve.h, m_u[g - 2])
            A.activation(sp_st[:, dsl], pdt[:], AF.Exp,
                         bias=bvec[:, dc : dc + 1], scale=1.0)
            m_sp[g] = act.inc(A.activation(dt_sb[:, dsl], sp_st[:, dsl],
                                            AF.Ln, bias=1.0, scale=1.0))

            V.wait_ge(act.h, m_sp[g])
            if g >= 2:
                V.wait_ge(pe.h, m_ytr.get(g - 2, 0))
                if USE_GPSIMD:
                    V.wait_ge(gp.h, m_gp[(g - 2, N - 1)])
            V.tensor_tensor(u_sb[:, dsl], dt_sb[:, dsl], xnT[:, tcol], Op.mult)
            V.scalar_tensor_tensor(y_sb[:, dsl], wsb[:], dvec[:, dc : dc + 1],
                                   xnT[:, tcol], Op.add, Op.mult)
            V.tensor_tensor(db_sb[:, 0:TB], u_sb[:, dsl], arena[:, 0:TB], Op.mult)
            m_u[g] = dve.inc(V.engine_nop())

            NGP = 16 if USE_GPSIMD else 0
            if USE_GPSIMD:
                G.wait_ge(dve.h, m_u[g])
                for k in (0, 1):
                    if (g - 1, k + 12) in m_scan:
                        G.wait_ge(dve.h, m_scan[(g - 1, k + 12)])
                    gi = G.tensor_tensor(db_sb[:, (k & 3) * TB : (k & 3) * TB + TB],
                                         u_sb[:, dsl],
                                         arena[:, k * TB : (k + 1) * TB], Op.mult)
                    m_gp[(g, k)] = gp.inc(gi)

            m_da = {}
            m_hc = {}
            m_hst = {}
            m_pair_g = {}
            for n in range(N):
                par = n & 1
                pda = (p6, p7)[par]
                psc = (p1, p3)[par]
                if n >= 2:
                    A.wait_ge(dve.h, m_scan[(g, n - 2)])
                else:
                    A.wait_ge(dve.h, m_u[g])
                m_da[n] = act.inc(A.activation(
                    pda[:], dt_sb[:, dsl], AF.Exp, bias=0.0,
                    scale=a_sb[:, dc * N + n : dc * N + n + 1]))
                if n >= 2:
                    A.wait_ge(dve.h, m_hc[n - 2])
                    m_hst[n - 2] = act.inc(A.activation(
                        hst[:, dc * N + n - 2 : dc * N + n - 1],
                        psc[:, TB - 1 : TB], AF.Copy, bias=0.0, scale=1.0))
                k = n + 2
                if k < N:
                    if k < NGP:
                        G.wait_ge(dve.h, m_scan[(g, k - 4)] if k >= 4 else
                                  m_u[g])
                        gi = G.tensor_tensor(
                            db_sb[:, (k & 3) * TB : (k & 3) * TB + TB],
                            u_sb[:, dsl], arena[:, k * TB : (k + 1) * TB],
                            Op.mult)
                        m_gp[(g, k)] = gp.inc(gi)
                    else:
                        V.tensor_tensor(
                            db_sb[:, (k & 3) * TB : (k & 3) * TB + TB],
                            u_sb[:, dsl], arena[:, k * TB : (k + 1) * TB],
                            Op.mult)
                if not USE_GPSIMD and n == 1:
                    V.tensor_tensor(db_sb[:, (n & 3) * TB : (n & 3) * TB + TB],
                                    u_sb[:, dsl],
                                    arena[:, n * TB : (n + 1) * TB], Op.mult)
                V.wait_ge(act.h, m_hst[n - 2] if n >= 2 else m_da[n])
                if n >= 2:
                    V.wait_ge(act.h, m_da[n])
                if n < NGP:
                    V.wait_ge(gp.h, m_gp[(g, n)])
                V.tensor_tensor_scan(psc[:],
                                     pda[:],
                                     db_sb[:, (n & 3) * TB : (n & 3) * TB + TB],
                                     hst[:, dc * N + n : dc * N + n + 1],
                                     Op.mult, Op.add)
                m_scan[(g, n)] = dve.inc(V.engine_nop())
                if USE_GPSIMD and (g - 1, 100 + n // 2) in m_gp:
                    V.wait_ge(gp.h, m_gp[(g - 1, 100 + n // 2)])
                V.tensor_tensor(h_all[:, n * TB : (n + 1) * TB], psc[:],
                                arena[:, (16 + n) * TB : (17 + n) * TB], Op.mult)
                m_hc[n] = dve.inc(V.engine_nop())
                if USE_GPSIMD and par == 1:
                    G.wait_ge(dve.h, m_hc[n])
                    gi = G.tensor_tensor(
                        h_all[:, (n - 1) * TB : n * TB],
                        h_all[:, (n - 1) * TB : n * TB],
                        h_all[:, n * TB : (n + 1) * TB], Op.add)
                    m_gp[(g, 100 + n // 2)] = gp.inc(gi)
            for n in (N - 2, N - 1):
                A.wait_ge(dve.h, m_hc[n])
                m_hst[n] = act.inc(A.activation(
                    hst[:, dc * N + n : dc * N + n + 1],
                    (p1, p3)[n & 1][:, TB - 1 : TB], AF.Copy, bias=0.0,
                    scale=1.0))
            m_hst_last = m_hst[N - 1]

            if USE_GPSIMD:
                V.wait_ge(gp.h, m_gp[(g, 100 + 7)])
                for a, b2 in ((0, 2), (4, 6), (8, 10), (12, 14), (0, 4),
                              (8, 12), (0, 8)):
                    V.tensor_tensor(h_all[:, a * TB : (a + 1) * TB],
                                    h_all[:, a * TB : (a + 1) * TB],
                                    h_all[:, b2 * TB : (b2 + 1) * TB], Op.add)
            else:
                w = 8
                while w >= 1:
                    V.tensor_tensor(h_all[:, 0 : w * TB], h_all[:, 0 : w * TB],
                                    h_all[:, w * TB : 2 * w * TB], Op.add)
                    w //= 2
            V.tensor_tensor(y_sb[:, dsl], y_sb[:, dsl], h_all[:, 0:TB], Op.add)
            V.wait_ge(pe.h, m_bm[g])
            V.tensor_tensor(y_sb[:, dsl], y_sb[:, dsl], pbm[:], Op.add)
            m_ydone[g] = dve.inc(V.engine_nop())

            if (g - 2) in m_oadd:
                SY.wait_ge(dve.h, m_oadd[g - 2])
            for tt in range(NTT):
                ld.inc(SY.dma_start(
                    xres[:, sl2 * TB + tt * 128 : sl2 * TB + (tt + 1) * 128],
                    bass.AP(d_xh, (r0 + tt * 128) * DH + dc * 128,
                            [[DH, 128], [1, 128]])), 16)
            m_ld_res = ld.v

            T.wait_ge(dve.h, m_ydone[g])
            if (g - 2) in m_oadd:
                T.wait_ge(dve.h, m_oadd[g - 2])
            for k in range(4):
                ti = T.transpose(
                    ptpy[:, k * 128 : (k + 1) * 128],
                    y_sb[:, sl2 * TB + k * 128 : sl2 * TB + (k + 1) * 128],
                    idf[:])
            m_ytr[g] = pe.inc(ti)
            V.wait_ge(pe.h, m_ytr[g])
            V.wait_ge(ld.h, m_ld_res)
            if (g - 2) in m_st:
                V.wait_ge(st.h, m_st[g - 2])
            V.tensor_tensor(outsb[:, dsl], xres[:, dsl], ptpy[:], Op.add)
            m_oadd[g] = dve.inc(V.engine_nop())
            SY.wait_ge(dve.h, m_oadd[g])
            for tt in range(NTT):
                st.inc(SY.dma_start(
                    bass.AP(d_out, (r0 + tt * 128) * DH + dc * 128,
                            [[DH, 128], [1, 128]]),
                    outsb[:, sl2 * TB + tt * 128 : sl2 * TB + (tt + 1) * 128]), 16)
            m_st[g] = st.v

        nc.all_engine_barrier()

    SY.wait_ge(st.h, st.v)
    SY.wait_ge(ld.h, ld.v)
    nc.all_engine_barrier()
    return nc


def kernel(L0_attn_x, L0_attn_A_log, L0_attn_D, L0_attn_x_proj_w,
           L0_attn_dt_proj_w, L0_attn_dt_proj_b, L0_ffn_router_weight,
           L0_ffn_router_bias, L0_ffn_expert_bias):
    x = np.asarray(L0_attn_x, dtype=np.float32)
    wcat_rows = np.concatenate([np.asarray(L0_attn_x_proj_w, np.float32),
                                np.asarray(L0_ffn_router_weight, np.float32)],
                               axis=0).T
    in_maps = []
    for c in range(8):
        b, dh = c >> 1, c & 1
        sl = slice(dh * DH, (dh + 1) * DH)
        perm = np.r_[dh * DH : (dh + 1) * DH, (1 - dh) * DH : (2 - dh) * DH]
        wc = wcat_rows[perm]
        wc = np.ascontiguousarray(
            wc.reshape(NDF, 128, 224).transpose(1, 0, 2).reshape(128, NDF * 224))
        in_maps.append(dict(
            x=np.ascontiguousarray(x[b][:, perm]),
            xh=np.ascontiguousarray(x[b][:, sl]),
            wcat=wc,
            dtw=np.ascontiguousarray(np.asarray(L0_attn_dt_proj_w, np.float32)[sl].T),
            alog=np.ascontiguousarray(
                np.asarray(L0_attn_A_log, np.float32)[sl]
                .reshape(NDC, 128, N).transpose(1, 0, 2).reshape(128, NDC * N)),
            dvec=np.ascontiguousarray(
                np.asarray(L0_attn_D, np.float32)[sl].reshape(NDC, 128).T),
            bvec=np.ascontiguousarray(
                np.asarray(L0_attn_dt_proj_b, np.float32)[sl].reshape(NDC, 128).T),
            rb=np.ascontiguousarray(
                np.asarray(L0_ffn_router_bias, np.float32)[:, None]),
            eb=np.ascontiguousarray(
                np.asarray(L0_ffn_expert_bias, np.float32)[:, sl]),
        ))
    nc = build_program()
    res = run_bass_kernel_spmd(nc, in_maps, core_ids=list(range(8)))
    out = np.empty((B, S, D), dtype=np.float32)
    for c in range(8):
        b, dh = c >> 1, c & 1
        out[b, :, dh * DH : (dh + 1) * DH] = res.results[c]["out"]
    return out
